# version 04
# baseline (speedup 1.0000x reference)
import numpy as np
import ml_dtypes
from contextlib import ExitStack
import concourse.bass as bass
import concourse.mybir as mybir
from concourse.bass_utils import run_bass_kernel_spmd

F32 = mybir.dt.float32
BF16 = mybir.dt.bfloat16
AF = mybir.ActivationFunctionType
ALU = mybir.AluOpType
BF = ml_dtypes.bfloat16

B, S, D = 8, 2048, 1024
H, HN = 16, 64
RANK, DFF = 256, 4096
NE, NI = 16, 512
NCORE = 8
ESL = D // NCORE
CH = 128
NCHUNK = S // CH
LN_EPS = 1e-5


class _Step:
    __slots__ = ("eng", "fn", "deps", "dma", "idx", "need_inc", "sem", "val", "g")

    def __init__(self, eng, fn, deps, dma, idx):
        self.eng, self.fn, self.deps, self.dma, self.idx = eng, fn, deps, dma, idx
        self.need_inc = False
        self.sem = None
        self.val = 0


class Buf:
    __slots__ = ("last_write", "last_reads")

    def __init__(self):
        self.last_write = None
        self.last_reads = []


class Prog:
    ENGS = ("sync", "scalar", "vector", "tensor", "gpsimd")
    DMA_POOL = 12

    def __init__(self):
        self.steps = {e: [] for e in self.ENGS}
        self.dma_count = {e: 0 for e in self.ENGS}
        self.gidx = 0
        self.last_dma = {}

    def add(self, eng, fn, reads=(), writes=(), deps=(), dma=False,
            fence=False):
        deps = list(deps)
        for b in reads:
            if b.last_write is not None:
                deps.append(b.last_write)
        for b in writes:
            if b.last_write is not None:
                deps.append(b.last_write)
            deps.extend(b.last_reads)
        if fence:
            same = [d for d in deps if d.eng == eng and not d.dma]
            if same:
                hop_eng = "sync" if eng != "sync" else "gpsimd"
                hop = self.add(hop_eng, lambda e: e.nop(), deps=same)
                deps = [d for d in deps
                        if d.dma or d.eng != eng] + [hop]
        st = _Step(eng, fn, deps, dma, len(self.steps[eng]))
        st.g = self.gidx
        self.gidx += 1
        self.steps[eng].append(st)
        for b in reads:
            b.last_reads.append(st)
        for b in writes:
            b.last_write = st
            b.last_reads = []
        if dma:
            st.sem = ("q", eng, self.dma_count[eng] % self.DMA_POOL)
            self.dma_count[eng] += 1
            prev = self.last_dma.get(st.sem)
            if prev is not None:
                st.deps.append(prev)
            self.last_dma[st.sem] = st
        else:
            st.sem = ("c", eng)
        return st

    def emit(self, nc, stack):
        for e in self.ENGS:
            for st in self.steps[e]:
                for d in st.deps:
                    if d.eng != st.eng or d.dma:
                        d.need_inc = True
                if st.dma:
                    st.need_inc = True
        cnt, sem_keys = {}, set()
        sem_hist = {}
        for e in self.ENGS:
            for st in sorted(self.steps[e], key=lambda x: x.g):
                if st.need_inc:
                    sem_keys.add(st.sem)
                    cnt[st.sem] = cnt.get(st.sem, 0) + (16 if st.dma else 1)
                    st.val = cnt[st.sem]
                    if st.dma:
                        sem_hist.setdefault(st.sem, []).append((st.g, st.val))

        import bisect

        def wait_val(dep, consumer):
            return dep.val
        sems = {k: stack.enter_context(
            nc.semaphore("s_" + "_".join(map(str, k)))) for k in sem_keys}
        block = stack.enter_context(nc.Block())
        prog = self

        def stream(eng_name):
            def run(eng):
                waited = {}
                for st in prog.steps[eng_name]:
                    for d in st.deps:
                        if d.eng == st.eng and not d.dma:
                            continue
                        v = wait_val(d, st)
                        if waited.get(d.sem, -1) >= v:
                            continue
                        eng.wait_ge(sems[d.sem], v)
                        waited[d.sem] = v
                    insts = st.fn(eng)
                    if st.need_inc:
                        inst = insts[-1] if isinstance(insts, (list, tuple)) \
                            else insts
                        assert inst is not None, f"step on {eng_name} needs inc"
                        inst.then_inc(sems[st.sem], 16 if st.dma else 1)
            return run

        for e in self.ENGS:
            getattr(block, e)(stream(e))


class Region:

    def __init__(self):
        self.bufs = {}

    def buf(self, key):
        return self.bufs.setdefault(key, Buf())

    def retire(self):
        steps = []
        for b in self.bufs.values():
            if b.last_write is not None:
                steps.append(b.last_write)
            steps.extend(b.last_reads)
        self.bufs = {}
        return steps


def _ln_emit(p, epsc_ap, s_ap, s_buf, out_ap, out_buf, sq_ap, sq_buf,
             stat_ap, stat_buf, extra_reads=()):
    D_ = 1024
    p.add("vector", lambda v: v.tensor_reduce(
        out=stat_ap[:, 0:1], in_=s_ap, axis=mybir.AxisListType.X, op=ALU.add),
        reads=[s_buf] + list(extra_reads), writes=[stat_buf], fence=True)
    p.add("scalar", lambda sc: sc.activation(
        out=sq_ap, in_=s_ap, func=AF.Square, accum_out=stat_ap[:, 2:3]),
        reads=[s_buf], writes=[sq_buf, stat_buf])
    p.add("scalar", lambda sc: sc.mul(
        out=stat_ap[:, 1:2], in_=stat_ap[:, 0:1], mul=1.0 / D_),
        reads=[stat_buf], writes=[stat_buf], fence=True)
    p.add("vector", lambda v: v.tensor_scalar(
        out=stat_ap[:, 3:4], in0=stat_ap[:, 2:3], scalar1=1.0 / D_,
        scalar2=None, op0=ALU.mult),
        reads=[stat_buf], writes=[stat_buf], fence=True)
    p.add("vector", lambda v: v.tensor_tensor(
        out=stat_ap[:, 4:5], in0=stat_ap[:, 1:2], in1=stat_ap[:, 1:2],
        op=ALU.mult), reads=[stat_buf], writes=[stat_buf], fence=True)
    p.add("gpsimd", lambda g: g.tensor_tensor(
        out=stat_ap[:, 5:6], in0=stat_ap[:, 3:4], in1=stat_ap[:, 4:5],
        op=ALU.subtract), reads=[stat_buf], writes=[stat_buf])
    p.add("scalar", lambda sc: sc.activation(
        out=stat_ap[:, 6:7], in_=stat_ap[:, 5:6], func=AF.Sqrt,
        bias=epsc_ap), reads=[stat_buf], writes=[stat_buf])
    p.add("vector", lambda v: v.reciprocal(
        out=stat_ap[:, 7:8], in_=stat_ap[:, 6:7]),
        reads=[stat_buf], writes=[stat_buf], fence=True)
    p.add("vector", lambda v: v.tensor_scalar(
        out=out_ap, in0=s_ap, scalar1=stat_ap[:, 1:2],
        scalar2=stat_ap[:, 7:8], op0=ALU.subtract, op1=ALU.mult),
        reads=[s_buf, stat_buf], writes=[out_buf], fence=True)


def build_l0(nc):
    ap = {}
    for name, shape in [
        ("x", [S, D]), ("wqk", [8, 128, 2048]), ("wv", [8, 128, 1024]),
        ("wo", [D, D]), ("u1", [D, RANK]), ("v1", [RANK, DFF]),
        ("b1t", [128, 32]), ("u2", [DFF, RANK]), ("v2", [RANK, D]),
        ("b2r", [1, D]), ("identf", [128, 128]), ("onesf", [1, 128]),
        ("selbcf", [65, 64]), ("epsc", [128, 1]),
    ]:
        ap[name] = nc.dram_tensor(name, shape, F32, kind="ExternalInput").ap()
    x2_out = nc.dram_tensor("x2", [S, D], F32, kind="ExternalOutput").ap()
    import os
    dbg = {}
    if os.environ.get("K_DEBUG"):
        for n, shp, dt in [("d_qk", [128, 4096], BF16), ("d_v", [128, 16640], BF16),
                           ("d_ct", [128, 16384], BF16), ("d_av", [65, 2048], F32),
                           ("d_xT", [128, 16384], BF16), ("d_s", [128, 1024], F32),
                           ("d_stat", [128, 16], F32)]:
            dbg[n] = nc.dram_tensor(n, shp, dt, kind="ExternalOutput").ap()

    st = ExitStack()
    E = st.enter_context
    BIGA = E(nc.sbuf_tensor("BIGA", [128, 16384], BF16))
    BIGB = E(nc.sbuf_tensor("BIGB", [128, 16384], BF16))
    BIGC = E(nc.sbuf_tensor("BIGC", [128, 16640], BF16))
    BIGD = E(nc.sbuf_tensor("BIGD", [128, 16384], BF16))
    WVO = E(nc.sbuf_tensor("WVO", [128, 8192], BF16))
    qk_b = E(nc.sbuf_tensor("qk_b", [128, 4096], BF16))
    stage_sb = [E(nc.sbuf_tensor(f"stage{i}", [64, 512], BF16)) for i in range(2)]
    selbc = E(nc.sbuf_tensor("selbc", [65, 64], F32))
    xb_t = [E(nc.sbuf_tensor(f"xb{i}", [128, 1024], BF16)) for i in range(2)]
    at_t = [E(nc.sbuf_tensor(f"at{i}", [128, 512], BF16)) for i in range(4)]
    av_sb = E(nc.sbuf_tensor("av_sb", [65, 2048], F32))
    h_t = [E(nc.sbuf_tensor(f"h{i}", [128, 512], BF16)) for i in range(2)]
    f32_t = [E(nc.sbuf_tensor(f"f32_{i}", [128, 1024], F32)) for i in range(6)]
    stat = E(nc.sbuf_tensor("stat", [128, 16], F32))
    identb = E(nc.sbuf_tensor("identb", [128, 128], BF16))
    onesb = E(nc.sbuf_tensor("onesb", [1, 128], F32))
    b1_sb = E(nc.sbuf_tensor("b1_sb", [128, 32], F32))
    b2bc = E(nc.sbuf_tensor("b2bc", [128, 1024], F32))
    eps_sb = E(nc.sbuf_tensor("eps_sb", [128, 1], F32))

    pA = [E(nc.psum_tensor(f"pA{i}", [128, 512], F32)) for i in range(4)]
    pB = [E(nc.psum_tensor(f"pB{i}", [128, 512], F32)) for i in range(2)]
    pC = [E(nc.psum_tensor(f"pC{i}", [128, 128], BF16)) for i in range(2)]

    p = Prog()
    RA, RB, RC, RD, RW = Region(), Region(), Region(), Region(), Region()
    Rp = Region()
    Rx2 = Region()
    Rf = Region()
    Rsm = Region()

    cidf = Rf.buf("sq")
    p.add("sync", lambda e: e.dma_start(out=onesb[:], in_=ap["onesf"]),
          writes=[Rsm.buf("onesb")], dma=True)
    ld_idf = p.add("sync", lambda e: e.dma_start(
        out=f32_t[4][:, 0:128], in_=ap["identf"]), writes=[cidf], dma=True)
    p.add("scalar", lambda s: s.copy(out=identb[:], in_=f32_t[4][:, 0:128]),
          reads=[cidf], writes=[Rsm.buf("identb")])
    p.add("sync", lambda e: e.dma_start(out=b1_sb[:], in_=ap["b1t"]),
          writes=[Rsm.buf("b1")], dma=True)
    p.add("sync", lambda e: e.dma_start(out=selbc[:], in_=ap["selbcf"]),
          writes=[Rsm.buf("selbc")], dma=True)
    p.add("sync", lambda e: e.dma_start(out=eps_sb[:], in_=ap["epsc"]),
          writes=[Rsm.buf("eps")], dma=True)
    p.add("sync", lambda e: e.dma_start(out=f32_t[5][0:1, :],
        in_=ap["b2r"]), writes=[Rsm.buf("b2r")], dma=True)
    for j in range(2):
        def mmb(t, j=j):
            return t.matmul(pB[j][:], onesb[:],
                            f32_t[5][0:1, j * 512:(j + 1) * 512],
                            start=True, stop=True, skip_group_check=True)
        mb = p.add("tensor", mmb, reads=[Rsm.buf("b2r"), Rsm.buf("onesb")],
                   writes=[Rp.buf(("pB", j))])
        p.add("scalar", lambda sc, j=j: sc.copy(
            out=b2bc[:, j * 512:(j + 1) * 512], in_=pB[j][:]),
            reads=[Rp.buf(("pB", j))], writes=[Rsm.buf("b2bc")])

    for dt in range(8):
        p.add("gpsimd", lambda g, dt=dt: g.dma_start(
            out=BIGA[:, dt * 2048:(dt + 1) * 2048], in_=ap["wqk"][dt]),
            writes=[RA.buf(("wqk", dt))], dma=True)
        p.add("gpsimd", lambda g, dt=dt: g.dma_start(
            out=WVO[:, dt * 1024:(dt + 1) * 1024], in_=ap["wv"][dt]),
            writes=[RW.buf(("wv", dt))], dma=True)

    def transpose_to(p, src_ap, src_buf, dst_ap, dst_buf, n128, pc_i=[0]):
        for k in range(n128):
            j = pc_i[0] % 2
            pc_i[0] += 1
            pcb = Rp.buf(("pC", j))
            p.add("tensor", lambda t, k=k, j=j: t.transpose(
                pC[j][:], src_ap[:, k * 128:(k + 1) * 128], identb[:]),
                reads=[src_buf, Rsm.buf("identb")], writes=[pcb])
            p.add("vector", lambda v, k=k, j=j: v.tensor_copy(
                out=dst_ap(k), in_=pC[j][:]), reads=[pcb], writes=[dst_buf])

    for tt in range(16):
        xbb = RB.buf(("xb", tt % 2))
        p.add("gpsimd", lambda g, tt=tt: g.dma_start(
            out=xb_t[tt % 2][:], in_=ap["x"][tt * 128:(tt + 1) * 128, :]),
            writes=[xbb], dma=True)
        transpose_to(p, xb_t[tt % 2], xbb,
                     (lambda tt: lambda dt: BIGB[:, dt * 2048 + tt * 128:
                                                 dt * 2048 + (tt + 1) * 128])(tt),
                     RB.buf("xT"), 8)

    xT = lambda dt: BIGB[:, dt * 2048:(dt + 1) * 2048]
    xTb = RB.buf("xT")

    for tt in range(16):
        p.add("vector", lambda v, tt=tt: v.memset(
            BIGC[:, tt * 1040 + 64:(tt + 1) * 1040:65], 1.0),
            writes=[RC.buf(("vones", tt))])
    for tt in range(16):
        pv = [Rp.buf(("pA", 2 * (tt % 2))), Rp.buf(("pA", 2 * (tt % 2) + 1))]
        for half in range(2):
            def mmv(t, tt=tt, half=half):
                i = []
                for dt in range(8):
                    i.append(t.matmul(
                        pA[2 * (tt % 2) + half][:],
                        xT(dt)[:, tt * 128:(tt + 1) * 128],
                        WVO[:, dt * 1024 + half * 512:dt * 1024 + (half + 1) * 512],
                        start=(dt == 0), stop=(dt == 7), skip_group_check=True))
                return i
            p.add("tensor", mmv, reads=[xTb] + [RW.buf(("wv", dt))
                                               for dt in range(8)],
                  writes=[pv[half]])
            def evv(v, tt=tt, half=half):
                return v.tensor_copy(
                    out=BIGC[:, tt * 1040 + half * 520:
                             tt * 1040 + (half + 1) * 520].rearrange(
                        "p (h c) -> p h c", c=65)[:, :, 0:64],
                    in_=pA[2 * (tt % 2) + half][:].rearrange(
                        "p (h c) -> p h c", c=64))
            p.add("vector", evv, reads=[pv[half]],
                  writes=[RC.buf(("v", tt, half))])

    v_tile = lambda tt: BIGC[:, tt * 1040:(tt + 1) * 1040]

    for pair in range(8):
        for a in range(2):
            pqk = [Rp.buf(("pA", c)) for c in range(4)]
            def mmqk(t, pair=pair, a=a):
                i = []
                for kt in range(8):
                    for c in range(4):
                        i.append(t.matmul(
                            pA[c][:],
                            BIGA[:, kt * 2048 + pair * 256 + a * 128:
                                 kt * 2048 + pair * 256 + (a + 1) * 128],
                            xT(kt)[:, c * 512:(c + 1) * 512],
                            start=(kt == 0), stop=(kt == 7),
                            skip_group_check=True))
                return i
            p.add("tensor", mmqk,
                  reads=[xTb] + [RA.buf(("wqk", kt)) for kt in range(8)],
                  writes=pqk)
            qkb = Rsm.buf(("qk", pair, a))
            for c in range(4):
                p.add("vector", lambda v, c=c, a=a: v.tensor_copy(
                    out=qk_b[:, a * 2048 + c * 512:a * 2048 + (c + 1) * 512],
                    in_=pA[c][:]), reads=[pqk[c]], writes=[qkb])

        for hp in range(2):
            h = pair * 2 + hp
            base = hp * 64
            qkb_q = Rsm.buf(("qk", pair, 0))
            qkb_k = Rsm.buf(("qk", pair, 1))
            pav = [Rp.buf(("pA", c)) for c in range(4)]
            for kt in range(16):
                for c in range(4):
                    psb = Rp.buf(("pB", c % 2))
                    p.add("tensor", lambda t, kt=kt, c=c, base=base: t.matmul(
                        pB[c % 2][:],
                        qk_b[base:base + 64, 2048 + kt * 128:2048 + (kt + 1) * 128],
                        qk_b[base:base + 64, c * 512:(c + 1) * 512],
                        start=True, stop=True, skip_group_check=True),
                        reads=[qkb_q, qkb_k], writes=[psb])
                    atb = Rsm.buf(("at", c))
                    p.add("scalar", lambda sc, c=c: sc.activation(
                        out=at_t[c][:], in_=pB[c % 2][:], func=AF.Exp,
                        scale=0.125), reads=[psb], writes=[atb])
                    p.add("tensor", lambda t, h=h, kt=kt, c=c: t.matmul(
                        pA[c][0:65, :],
                        v_tile(kt)[:, h * 65:(h + 1) * 65],
                        at_t[c][:],
                        start=(kt == 0), stop=(kt == 15),
                        skip_group_check=True),
                        reads=[atb, RC.buf(("v", kt, h // 8)),
                               RC.buf(("vones", kt))],
                        writes=[pav[c]])
            ct = h // 2
            avb = Rsm.buf("av_sb")
            for c in range(4):
                p.add("vector", lambda v, c=c: v.tensor_copy(
                    out=av_sb[:, c * 512:(c + 1) * 512], in_=pA[c][0:65, :]),
                    reads=[pav[c]], writes=[avb])
            p.add("vector", lambda v: v.reciprocal(
                out=av_sb[64:65, :], in_=av_sb[64:65, :]),
                reads=[avb], writes=[avb], fence=True)
            for c in range(4):
                pdv = Rp.buf(("pB", c % 2))
                p.add("tensor", lambda t, c=c: t.matmul(
                    pB[c % 2][0:64, :], selbc[:],
                    av_sb[:, c * 512:(c + 1) * 512], start=True, stop=True,
                    skip_group_check=True),
                    reads=[avb, Rsm.buf("selbc")], writes=[pdv])
                if hp == 0:
                    p.add("vector", lambda v, c=c, ct=ct: v.tensor_tensor(
                        out=BIGD[0:64, ct * 2048 + c * 512:
                                 ct * 2048 + (c + 1) * 512],
                        in0=av_sb[0:64, c * 512:(c + 1) * 512],
                        in1=pB[c % 2][0:64, :], op=ALU.mult),
                        reads=[avb, pdv], writes=[RD.buf(("cT", ct))])
                else:
                    stb = Rsm.buf(("stage", c % 2))
                    p.add("vector", lambda v, c=c: v.tensor_tensor(
                        out=stage_sb[c % 2][:],
                        in0=av_sb[0:64, c * 512:(c + 1) * 512],
                        in1=pB[c % 2][0:64, :], op=ALU.mult),
                        reads=[avb, pdv], writes=[stb])
                    p.add("gpsimd", lambda g, c=c, ct=ct: g.dma_start(
                        out=BIGD[64:128, ct * 2048 + c * 512:
                                 ct * 2048 + (c + 1) * 512],
                        in_=stage_sb[c % 2][:]),
                        reads=[stb], writes=[RD.buf(("cT", ct))], dma=True)

    wvo_dead = RW.retire()
    for ct in range(8):
        p.add("gpsimd", lambda g, ct=ct: g.dma_start(
            out=WVO[:, ct * 1024:(ct + 1) * 1024],
            in_=ap["wo"][ct * 128:(ct + 1) * 128, :]),
            writes=[RW.buf(("wo", ct))], deps=wvo_dead, dma=True)

    store_steps = []
    for tt in range(16):
        po = [Rp.buf(("pA", 2 * (tt % 2) + k)) for k in range(2)]
        for half in range(2):
            def mmo(t, tt=tt, half=half):
                i = []
                for ct in range(8):
                    i.append(t.matmul(
                        pA[2 * (tt % 2) + half][:],
                        BIGD[:, ct * 2048 + tt * 128:ct * 2048 + (tt + 1) * 128],
                        WVO[:, ct * 1024 + half * 512:ct * 1024 + (half + 1) * 512],
                        start=(ct == 0), stop=(ct == 7), skip_group_check=True))
                return i
            p.add("tensor", mmo,
                  reads=[RD.buf(("cT", c2)) for c2 in range(8)] +
                        [RW.buf(("wo", c2)) for c2 in range(8)],
                  writes=[po[half]])
        xrb = Rf.buf(("xr", tt % 2))
        p.add("sync", lambda e, tt=tt: e.dma_start(
            out=f32_t[tt % 2][:], in_=ap["x"][tt * 128:(tt + 1) * 128, :]),
            writes=[xrb], dma=True)
        sb = Rf.buf(("s", tt % 2))
        def addres(v, tt=tt):
            i = []
            for half in range(2):
                i.append(v.tensor_tensor(
                    out=f32_t[2 + tt % 2][:, half * 512:(half + 1) * 512],
                    in0=f32_t[tt % 2][:, half * 512:(half + 1) * 512],
                    in1=pA[2 * (tt % 2) + half][:], op=ALU.add))
            return i
        p.add("vector", addres, reads=[xrb] + po, writes=[sb])
        if dbg and tt == 0:
            p.add("sync", lambda e: e.dma_start(out=dbg["d_s"],
                  in_=f32_t[2][:]), reads=[sb], dma=True)
        outb = xrb
        _ln_emit(p, eps_sb[:], f32_t[2 + tt % 2][:], sb,
                 f32_t[tt % 2][:], outb,
                 f32_t[4][:], Rf.buf("sq"),
                 stat[:, (tt % 2) * 8:(tt % 2) * 8 + 8], Rf.buf(("st", tt % 2)),
                 extra_reads=[Rsm.buf("eps")])
        if dbg and tt == 0:
            p.add("sync", lambda e: e.dma_start(out=dbg["d_stat"],
                  in_=stat[:]), reads=[Rf.buf(("st", 0))], dma=True)
        stc = p.add("sync", lambda e, tt=tt: e.dma_start(
            out=x2_out[tt * 128:(tt + 1) * 128, :],
            in_=f32_t[tt % 2][:]), reads=[outb], writes=[Rx2.buf(tt)],
            dma=True)
        store_steps.append(stc)

    if dbg:
        dsteps = []
        dsteps.append(p.add("sync", lambda e: e.dma_start(out=dbg["d_qk"], in_=qk_b[:]),
              reads=[Rsm.buf(("qk", 7, 0)), Rsm.buf(("qk", 7, 1))], dma=True))
        dsteps.append(p.add("sync", lambda e: e.dma_start(out=dbg["d_v"], in_=BIGC[:]),
              reads=[RC.buf(("v", tt2, hf)) for tt2 in range(16) for hf in range(2)], dma=True))
        dsteps.append(p.add("sync", lambda e: e.dma_start(out=dbg["d_ct"], in_=BIGD[:]),
              reads=[RD.buf(("cT", c2)) for c2 in range(8)], dma=True))
        dsteps.append(p.add("sync", lambda e: e.dma_start(out=dbg["d_av"], in_=av_sb[:]),
              reads=[Rsm.buf("av_sb")], dma=True))
        dsteps.append(p.add("sync", lambda e: e.dma_start(out=dbg["d_xT"], in_=BIGB[:]),
              reads=[RB.buf("xT")], dma=True))
        store_steps += dsteps
    ra_dead = RA.retire()
    rb_dead = RB.retire()
    rc_dead = RC.retire()
    rd_dead = RD.retire()

    def fresh(region, key, dead):
        b = region.buf(key)
        b.last_reads = list(dead)
        return b
    for k in [("PT", 0), ("PT", 1), ("QT", 0), ("QT", 1)]:
        fresh(RC, k, rc_dead)
    fresh(RB, "xT2", rb_dead)
    p.add("gpsimd", lambda g: g.dma_start(
        out=BIGC[:, 0:2048],
        in_=ap["u1"].rearrange("(kt p) r -> p kt r", p=128)),
        writes=[RC.buf("u1")], deps=rc_dead, dma=True)
    p.add("gpsimd", lambda g: g.dma_start(
        out=BIGC[:, 2048:4096],
        in_=ap["v2"].rearrange("(kt p) r -> p kt r", p=128)),
        writes=[RC.buf("v2")], deps=rc_dead, dma=True)
    p.add("gpsimd", lambda g: g.dma_start(
        out=BIGD[:, 0:8192],
        in_=ap["v1"].rearrange("(kt p) f -> p kt f", p=128)),
        writes=[RD.buf("v1")], deps=rd_dead, dma=True)
    p.add("gpsimd", lambda g: g.dma_start(
        out=BIGD[:, 8192:16384],
        in_=ap["u2"].rearrange("(kt p) r -> p kt r", p=128)),
        writes=[RD.buf("u2")], deps=rd_dead, dma=True)

    for tt in range(16):
        xbb = RB.buf(("xb2", tt % 2))
        p.add("gpsimd", lambda g, tt=tt: g.dma_start(
            out=xb_t[tt % 2][:], in_=x2_out[tt * 128:(tt + 1) * 128, :]),
            reads=[Rx2.buf(tt)], writes=[xbb],
            deps=(rb_dead if tt < 2 else ()), dma=True)
        transpose_to(p, xb_t[tt % 2], xbb,
                     (lambda tt: lambda dt: BIGB[:, dt * 2048 + tt * 128:
                                                 dt * 2048 + (tt + 1) * 128])(tt),
                     RB.buf("xT2"), 8)
    xT2 = lambda dt: BIGB[:, dt * 2048:(dt + 1) * 2048]
    xT2b = RB.buf("xT2")

    for mh in range(2):
        ppt = [Rp.buf(("pA", c)) for c in range(4)]
        def mmpt(t, mh=mh):
            i = []
            for kt in range(8):
                for c in range(4):
                    i.append(t.matmul(
                        pA[c][:],
                        BIGC[:, kt * 256 + mh * 128:kt * 256 + (mh + 1) * 128],
                        xT2(kt)[:, c * 512:(c + 1) * 512],
                        start=(kt == 0), stop=(kt == 7),
                        skip_group_check=True))
            return i
        p.add("tensor", mmpt, reads=[xT2b, RC.buf("u1")], writes=ppt)
        for c in range(4):
            p.add("vector", lambda v, mh=mh, c=c: v.tensor_copy(
                out=BIGC[:, 4096 + mh * 2048 + c * 512:
                         4096 + mh * 2048 + (c + 1) * 512],
                in_=pA[c][:]), reads=[ppt[c]], writes=[RC.buf(("PT", mh))])

    for tc in range(4):
        pqt = [Rp.buf(("pA", m2)) for m2 in range(2)]
        for mt in range(32):
            phb = Rp.buf(("pB", mt % 2))
            def mmh(t, tc=tc, mt=mt):
                i = []
                for rt in range(2):
                    i.append(t.matmul(
                        pB[mt % 2][:],
                        BIGD[:, rt * 4096 + mt * 128:rt * 4096 + (mt + 1) * 128],
                        BIGC[:, 4096 + rt * 2048 + tc * 512:
                             4096 + rt * 2048 + (tc + 1) * 512],
                        start=(rt == 0), stop=(rt == 1),
                        skip_group_check=True))
                return i
            p.add("tensor", mmh, reads=[RD.buf("v1"), RC.buf(("PT", 0)),
                                        RC.buf(("PT", 1))], writes=[phb])
            hb = Rsm.buf(("h", mt % 2))
            p.add("scalar", lambda sc, mt=mt: sc.activation(
                out=h_t[mt % 2][:], in_=pB[mt % 2][:],
                func=AF.Gelu_apprx_tanh, bias=b1_sb[:, mt:mt + 1]),
                reads=[phb, Rsm.buf("b1")], writes=[hb])
            def mmqt(t, mt=mt):
                i = []
                for m2 in range(2):
                    i.append(t.matmul(
                        pA[m2][:],
                        BIGD[:, 8192 + mt * 256 + m2 * 128:
                             8192 + mt * 256 + (m2 + 1) * 128],
                        h_t[mt % 2][:],
                        start=(mt == 0), stop=(mt == 31),
                        skip_group_check=True))
                return i
            p.add("tensor", mmqt, reads=[hb, RD.buf("u2")], writes=pqt)
        for m2 in range(2):
            p.add("vector", lambda v, tc=tc, m2=m2: v.tensor_copy(
                out=BIGC[:, 8192 + m2 * 2048 + tc * 512:
                         8192 + m2 * 2048 + (tc + 1) * 512],
                in_=pA[m2][:]), reads=[pqt[m2]], writes=[RC.buf(("QT", m2))])

    for tt in range(16):
        po = [Rp.buf(("pA", 2 + half)) for half in range(2)]
        for half in range(2):
            def mmf(t, tt=tt, half=half):
                i = []
                for rt2 in range(2):
                    i.append(t.matmul(
                        pA[2 + half][:],
                        BIGC[:, 8192 + rt2 * 2048 + tt * 128:
                             8192 + rt2 * 2048 + (tt + 1) * 128],
                        BIGC[:, 2048 + rt2 * 1024 + half * 512:
                             2048 + rt2 * 1024 + (half + 1) * 512],
                        start=(rt2 == 0), stop=(rt2 == 1),
                        skip_group_check=True))
                return i
            p.add("tensor", mmf, reads=[RC.buf(("QT", 0)), RC.buf(("QT", 1)),
                                        RC.buf("v2")], writes=[po[half]])
        xrb = Rf.buf(("xr", tt % 2))
        p.add("sync", lambda e, tt=tt: e.dma_start(
            out=f32_t[tt % 2][:], in_=x2_out[tt * 128:(tt + 1) * 128, :]),
            reads=[Rx2.buf(tt)], writes=[xrb], dma=True)
        prtb = Rf.buf(("prt", tt % 2))
        p.add("gpsimd", lambda g, tt=tt: g.tensor_tensor(
            out=f32_t[2 + tt % 2][:], in0=f32_t[tt % 2][:], in1=b2bc[:],
            op=ALU.add), reads=[xrb, Rsm.buf("b2bc")], writes=[prtb])
        sb = xrb
        def adds(v, tt=tt):
            i = []
            for half in range(2):
                i.append(v.tensor_tensor(
                    out=f32_t[tt % 2][:, half * 512:(half + 1) * 512],
                    in0=f32_t[2 + tt % 2][:, half * 512:(half + 1) * 512],
                    in1=pA[2 + half][:], op=ALU.add))
            return i
        p.add("vector", adds, reads=[prtb] + po, writes=[sb])
        _ln_emit(p, eps_sb[:], f32_t[tt % 2][:], sb,
                 f32_t[2 + tt % 2][:], prtb,
                 f32_t[4][:], Rf.buf("sq"),
                 stat[:, (tt % 2) * 8:(tt % 2) * 8 + 8], Rf.buf(("st", tt % 2)),
                 extra_reads=[Rsm.buf("eps")])
        stc = p.add("sync", lambda e, tt=tt: e.dma_start(
            out=x2_out[tt * 128:(tt + 1) * 128, :],
            in_=f32_t[2 + tt % 2][:]), reads=[prtb], writes=[Rx2.buf(tt)],
            dma=True)
        store_steps.append(stc)

    p.add("gpsimd", lambda e: None, deps=store_steps)
    return ap, x2_out, p, st


def _prep_l0(inputs, b):
    wqkv = np.asarray(inputs["L0_attn_w_qkv"])
    m = {
        "x": np.ascontiguousarray(np.asarray(inputs["L0_attn_inputs"])[b]),
        "wqk": np.ascontiguousarray(
            wqkv[:, :2].transpose(2, 0, 1, 3)
            .reshape(1024, 8, 2, 2, 64)
            .transpose(0, 1, 3, 2, 4)
            .reshape(8, 128, 2048)),
        "wv": np.ascontiguousarray(
            wqkv[:, 2].transpose(1, 0, 2).reshape(8, 128, 1024)),
        "wo": np.ascontiguousarray(np.asarray(inputs["L0_attn_w_o"])),
        "u1": np.ascontiguousarray(np.asarray(inputs["L0_ffn_U1"])),
        "v1": np.ascontiguousarray(np.asarray(inputs["L0_ffn_V1"])),
        "b1t": np.ascontiguousarray(
            np.asarray(inputs["L0_ffn_b1"]).reshape(32, 128).T),
        "u2": np.ascontiguousarray(np.asarray(inputs["L0_ffn_U2"])),
        "v2": np.ascontiguousarray(np.asarray(inputs["L0_ffn_V2"])),
        "b2r": np.ascontiguousarray(
            np.asarray(inputs["L0_ffn_b2"]).reshape(1, 1024)),
        "identf": np.eye(128, dtype=np.float32),
        "selbcf": np.concatenate([np.zeros((64, 64), np.float32),
                                  np.ones((1, 64), np.float32)], axis=0),
        "epsc": np.full((128, 1), 1e-5, dtype=np.float32),
        "onesf": np.ones((1, 128), dtype=np.float32),
    }
    return {k: np.ascontiguousarray(v.astype(np.float32, copy=False))
            for k, v in m.items()}


def build_l2a(nc):
    ap = {}
    for name, shape in [("xcT", [2, D, 1024]), ("eta", [1, 1]),
                        ("identf", [128, 128]), ("onesf", [1, 128]),
                        ("self", [8, 2048])]:
        ap[name] = nc.dram_tensor(name, shape, F32, kind="ExternalInput").ap()
    b_out = nc.dram_tensor("Bout", [2, 1024, 1024], BF16,
                           kind="ExternalOutput").ap()
    m_out = nc.dram_tensor("Mout", [2, 8, 128, 128], BF16,
                           kind="ExternalOutput").ap()
    import os
    dbg = {}
    if os.environ.get("K_DEBUG"):
        for n, shp, dt in [("d_msb", [128, 128], BF16),
                           ("d_idb", [128, 128], BF16),
                           ("d_stg", [8, 128], BF16)]:
            dbg[n] = nc.dram_tensor(n, shp, dt, kind="ExternalOutput").ap()

    st = ExitStack()
    E = st.enter_context
    XcT = [E(nc.sbuf_tensor(f"XcT{i}", [128, 8192], BF16)) for i in range(2)]
    B_sb = [E(nc.sbuf_tensor(f"Bsb{i}", [128, 8192], BF16)) for i in range(2)]
    M_sb = [E(nc.sbuf_tensor(f"Msb{i}", [128, 128], BF16)) for i in range(2)]
    Mo_sb = [E(nc.sbuf_tensor(f"Mo{i}", [128, 128], BF16)) for i in range(2)]
    stage8 = [E(nc.sbuf_tensor(f"stg{i}", [8, 128], BF16)) for i in range(2)]
    sel_b = E(nc.sbuf_tensor("sel_b", [8, 2048], BF16))
    identb = E(nc.sbuf_tensor("identb", [128, 128], BF16))
    misc = E(nc.sbuf_tensor("misc", [128, 2304], F32))
    mcvec = E(nc.sbuf_tensor("mcvec", [128, 1], F32))
    ones_sb = E(nc.sbuf_tensor("ones_sb", [1, 128], F32))
    eta_sb = E(nc.sbuf_tensor("eta_sb", [1, 1], F32))

    pA = [E(nc.psum_tensor(f"pA{i}", [128, 512], F32)) for i in range(2)]
    pM = [E(nc.psum_tensor(f"pM{i}", [128, 128], F32)) for i in range(2)]
    pS = [E(nc.psum_tensor(f"pS{i}", [8, 128], F32)) for i in range(2)]
    pE1 = E(nc.psum_tensor("pE1", [128, 1], F32))
    pC = E(nc.psum_tensor("pC", [128, 128], BF16))

    p = Prog()
    R = Region()
    Rp = Region()

    p.add("sync", lambda e: e.dma_start(out=ones_sb[:], in_=ap["onesf"]),
          writes=[R.buf("ones")], dma=True)
    p.add("sync", lambda e: e.dma_start(out=eta_sb[:], in_=ap["eta"]),
          writes=[R.buf("eta")], dma=True)
    p.add("sync", lambda e: e.dma_start(out=misc[:, 0:128], in_=ap["identf"]),
          writes=[R.buf("mi")], dma=True)
    p.add("scalar", lambda s: s.copy(out=identb[:], in_=misc[:, 0:128]),
          reads=[R.buf("mi")], writes=[R.buf("identb")])
    p.add("sync", lambda e: e.dma_start(out=misc[0:8, 256:2304], in_=ap["self"]),
          writes=[R.buf("ms")], dma=True)
    p.add("scalar", lambda s: s.copy(out=sel_b[:], in_=misc[0:8, 256:2304]),
          reads=[R.buf("ms")], writes=[R.buf("sel")])
    p.add("tensor", lambda t: t.matmul(pE1[:], ones_sb[:], eta_sb[:],
                                       start=True, stop=True,
                                       skip_group_check=True),
          reads=[R.buf("ones"), R.buf("eta")], writes=[Rp.buf("pE1")])
    p.add("scalar", lambda sc: sc.mul(
        out=mcvec[:], in_=pE1[:], mul=-1.0 / 4096.0),
        reads=[Rp.buf("pE1")], writes=[R.buf("mc")])

    for j in range(2):
        xb = R.buf(("xcT", j))
        p.add("gpsimd", lambda g, j=j: g.dma_start(
            out=XcT[j][:],
            in_=ap["xcT"][j].rearrange("(kt p) r -> p kt r", p=128)),
            writes=[xb], dma=True)
        for mt in range(8):
            for c2 in range(2):
                pb = Rp.buf(("pA", (2 * mt + c2) % 2))
                def mma(t, j=j, mt=mt, c2=c2):
                    i = []
                    for kt in range(8):
                        i.append(t.matmul(
                            pA[(2 * mt + c2) % 2][:],
                            XcT[j][:, kt * 1024 + mt * 128:
                                 kt * 1024 + (mt + 1) * 128],
                            XcT[j][:, kt * 1024 + c2 * 512:
                                 kt * 1024 + (c2 + 1) * 512],
                            start=(kt == 0), stop=(kt == 7),
                            skip_group_check=True))
                    return i
                p.add("tensor", mma, reads=[xb], writes=[pb])
                p.add("vector", lambda v, j=j, mt=mt, c2=c2: v.tensor_scalar(
                    out=B_sb[j][:, mt * 1024 + c2 * 512:
                               mt * 1024 + (c2 + 1) * 512],
                    in0=pA[(2 * mt + c2) % 2][:], scalar1=mcvec[:],
                    scalar2=None, op0=ALU.mult),
                    reads=[pb, R.buf("mc")], writes=[R.buf(("B", j))])
        p.add("sync", lambda e, j=j: e.dma_start(
            out=b_out[j].rearrange("(mt p) n -> p mt n", p=128),
            in_=B_sb[j][:]), reads=[R.buf(("B", j))], dma=True)

        for i in range(8):
            m_i = (j * 8 + i) % 2
            mb = R.buf(("M", m_i))
            pmb = Rp.buf(("pM", m_i))
            p.add("vector", lambda v, m_i=m_i: v.tensor_copy(
                out=M_sb[m_i][:], in_=identb[:]),
                reads=[R.buf("identb")], writes=[mb])
            dref = B_sb[j][:, i * 1024 + i * 128:i * 1024 + (i + 1) * 128]
            for s_ in range(1, 16):
                psb = Rp.buf(("pS", s_ % 2))
                p.add("tensor", lambda t, s_=s_, dref=dref, m_i=m_i: t.matmul(
                    pS[s_ % 2][0:8, 0:8 * s_],
                    dref[0:8 * s_, s_ * 8:(s_ + 1) * 8],
                    M_sb[m_i][0:8 * s_, 0:8 * s_],
                    start=True, stop=True, skip_group_check=True),
                    reads=[R.buf(("B", j)), mb], writes=[psb])
                sgb = R.buf(("stg", s_ % 2))
                p.add("vector", lambda v, s_=s_: v.tensor_copy(
                    out=stage8[s_ % 2][0:8, 0:8 * s_],
                    in_=pS[s_ % 2][0:8, 0:8 * s_]),
                    reads=[psb], writes=[sgb])
                p.add("tensor", lambda t, s_=s_, m_i=m_i: t.matmul(
                    pM[m_i][0:128, 0:8 * s_],
                    sel_b[:, s_ * 128:(s_ + 1) * 128],
                    stage8[s_ % 2][0:8, 0:8 * s_],
                    start=(s_ == 1), stop=(s_ == 15),
                    skip_group_check=True),
                    reads=[sgb, R.buf("sel")], writes=[pmb])
                p.add("vector", lambda v, s_=s_, m_i=m_i: v.tensor_tensor(
                    out=M_sb[m_i][:, 0:8 * s_], in0=pM[m_i][0:128, 0:8 * s_],
                    in1=identb[:, 0:8 * s_], op=ALU.add),
                    reads=[pmb, R.buf("identb")], writes=[mb], fence=True)
            if dbg and j == 0 and i == 0:
                p.add("sync", lambda e, m_i=m_i: e.dma_start(
                    out=dbg["d_msb"], in_=M_sb[m_i][:]), reads=[mb], dma=True)
                p.add("sync", lambda e: e.dma_start(
                    out=dbg["d_idb"], in_=identb[:]),
                    reads=[R.buf("identb")], dma=True)
                p.add("sync", lambda e: e.dma_start(
                    out=dbg["d_stg"], in_=stage8[1][:]),
                    reads=[R.buf(("stg", 1))], dma=True)
            pcb = Rp.buf("pC")
            p.add("tensor", lambda t, m_i=m_i: t.transpose(
                pC[:], M_sb[m_i][:], identb[:]),
                reads=[mb, R.buf("identb")], writes=[pcb])
            mo = R.buf(("Mo", m_i))
            p.add("vector", lambda v, m_i=m_i: v.tensor_copy(
                out=Mo_sb[m_i][:], in_=pC[:]), reads=[pcb], writes=[mo])
            p.add("sync", lambda e, j=j, i=i, m_i=m_i: e.dma_start(
                out=m_out[j, i], in_=Mo_sb[m_i][:]), reads=[mo], dma=True)

    outs = [s2 for s2 in p.steps["sync"] if s2.dma]
    p.add("gpsimd", lambda e: None, deps=outs)
    return ap, p, st


def _prep_l2a(x2_full, inputs, k):
    sel = np.zeros((8, 16, 128), np.float32)
    for s_ in range(1, 16):
        for kk in range(8):
            sel[kk, s_, 8 * s_ + kk] = 1.0
    xcT = np.stack([
        np.ascontiguousarray(
            x2_full[:, c * 128:(c + 1) * 128, :].transpose(2, 1, 0)
            .reshape(1024, 1024))
        for c in (2 * k, 2 * k + 1)])
    return {
        "xcT": xcT.astype(np.float32),
        "eta": np.asarray(inputs["L1_attn_eta"], np.float32).reshape(1, 1),
        "identf": np.eye(128, dtype=np.float32),
        "onesf": np.ones((1, 128), np.float32),
        "self": np.ascontiguousarray(sel.reshape(8, 2048)),
    }


def build_l2b(nc):
    ap = {}
    for name, shape, dt in [
        ("xcT", [NCHUNK, D, 1024], F32), ("xcr", [NCHUNK, 1024, D], F32),
        ("Ball", [NCHUNK, 1024, 1024], BF16), ("Mall", [NCHUNK, 8, 128, 128], BF16),
        ("w0t", [D, ESL], F32), ("wvt", [D, ESL], F32), ("bvr", [1, ESL], F32),
        ("eta", [1, 1], F32), ("onesf", [1, 128], F32),
    ]:
        ap[name] = nc.dram_tensor(name, shape, dt, kind="ExternalInput").ap()
    y_out = nc.dram_tensor("yout", [NCHUNK, 1024, ESL], F32,
                           kind="ExternalOutput").ap()

    st = ExitStack()
    E = st.enter_context
    XcT = [E(nc.sbuf_tensor(f"XcT{i}", [128, 8192], BF16)) for i in range(2)]
    Xcr = [E(nc.sbuf_tensor(f"Xcr{i}", [128, 8192], BF16)) for i in range(2)]
    B_sb = [E(nc.sbuf_tensor(f"Bsb{i}", [128, 8192], BF16)) for i in range(2)]
    M_sb = [E(nc.sbuf_tensor(f"Msb{i}", [128, 1024], BF16)) for i in range(2)]
    W_f32 = E(nc.sbuf_tensor("W_f32", [128, 1024], F32))
    Wt_b = E(nc.sbuf_tensor("Wt_b", [128, 1024], BF16))
    Wv_b = E(nc.sbuf_tensor("Wv_b", [128, 1024], BF16))
    V_f32 = E(nc.sbuf_tensor("V_f32", [128, 1024], F32))
    R_f32 = E(nc.sbuf_tensor("R_f32", [128, 1024], F32))
    E_b = E(nc.sbuf_tensor("E_b", [128, 1024], BF16))
    Y_f32 = [E(nc.sbuf_tensor(f"Y{i}", [128, 1024], F32)) for i in range(2)]
    G_b = [E(nc.sbuf_tensor(f"G{i}", [128, 128], BF16)) for i in range(2)]
    bvbc = E(nc.sbuf_tensor("bvbc", [128, 128], F32))
    mcvec = E(nc.sbuf_tensor("mcvec", [128, 1], F32))
    ones_sb = E(nc.sbuf_tensor("ones_sb", [1, 128], F32))
    eta_sb = E(nc.sbuf_tensor("eta_sb", [1, 1], F32))
    bv_sb = E(nc.sbuf_tensor("bv_sb", [1, 128], F32))

    pZ = [E(nc.psum_tensor(f"pZ{i}", [128, 128], F32)) for i in range(2)]
    pX = E(nc.psum_tensor("pX", [128, 128], F32))
    pE = E(nc.psum_tensor("pE", [128, 128], F32))
    pW = [E(nc.psum_tensor(f"pW{i}", [128, 128], F32)) for i in range(2)]
    pB1 = E(nc.psum_tensor("pB1", [128, 128], F32))

    p = Prog()
    R = Region()
    Rp = Region()

    p.add("sync", lambda e: e.dma_start(out=ones_sb[:], in_=ap["onesf"]),
          writes=[R.buf("ones")], dma=True)
    p.add("sync", lambda e: e.dma_start(out=eta_sb[:], in_=ap["eta"]),
          writes=[R.buf("eta")], dma=True)
    p.add("sync", lambda e: e.dma_start(out=bv_sb[:], in_=ap["bvr"]),
          writes=[R.buf("bv")], dma=True)
    p.add("sync", lambda e: e.dma_start(
        out=W_f32[:], in_=ap["w0t"].rearrange("(kt p) e -> p kt e", p=128)),
        writes=[R.buf("W")], dma=True)
    p.add("scalar", lambda s: s.copy(out=Wt_b[:], in_=W_f32[:]),
          reads=[R.buf("W")], writes=[R.buf("Wb")])
    p.add("gpsimd", lambda g: g.dma_start(
        out=Wv_b[:], in_=ap["wvt"].rearrange("(kt p) e -> p kt e", p=128)),
        writes=[R.buf("Wv")], dma=True)
    p.add("tensor", lambda t: t.matmul(pB1[:, 0:1], ones_sb[:], eta_sb[:],
                                       start=True, stop=True,
                                       skip_group_check=True),
          reads=[R.buf("ones"), R.buf("eta")], writes=[Rp.buf("pB1")])
    p.add("scalar", lambda sc: sc.mul(
        out=mcvec[:], in_=pB1[:, 0:1], mul=-1.0 / 4096.0),
        reads=[Rp.buf("pB1")], writes=[R.buf("mc")])
    p.add("tensor", lambda t: t.matmul(pB1[:], ones_sb[:], bv_sb[:],
                                       start=True, stop=True,
                                       skip_group_check=True),
          reads=[R.buf("bv"), R.buf("ones"), Rp.buf("pB1"), R.buf("mc")],
          writes=[Rp.buf("pB1")])
    p.add("scalar", lambda sc: sc.copy(out=bvbc[:], in_=pB1[:]),
          reads=[Rp.buf("pB1")], writes=[R.buf("bvbc")])

    store_steps = []
    for c in range(NCHUNK):
        par = c % 2
        xtb, xrb2, bb, mb = (R.buf(("xcT", par)), R.buf(("xcr", par)),
                             R.buf(("B", par)), R.buf(("M", par)))
        p.add("gpsimd", lambda g, c=c, par=par: g.dma_start(
            out=XcT[par][:],
            in_=ap["xcT"][c].rearrange("(kt p) r -> p kt r", p=128)),
            writes=[xtb], dma=True)
        p.add("gpsimd", lambda g, c=c, par=par: g.dma_start(
            out=Xcr[par][:],
            in_=ap["xcr"][c].rearrange("(rt p) d -> p rt d", p=128)),
            writes=[xrb2], dma=True)
        p.add("sync", lambda e, c=c, par=par: e.dma_start(
            out=B_sb[par][:],
            in_=ap["Ball"][c].rearrange("(mt p) n -> p mt n", p=128)),
            writes=[bb], dma=True)
        p.add("sync", lambda e, c=c, par=par: e.dma_start(
            out=M_sb[par][:],
            in_=ap["Mall"][c].rearrange("i p f -> p i f")),
            writes=[mb], dma=True)

        vb = R.buf("V")
        rb = R.buf("R")
        for rt in range(8):
            pzb = Rp.buf(("pZ", rt % 2))
            def mmv(t, par=par, rt=rt):
                i = []
                for kt in range(8):
                    i.append(t.matmul(
                        pZ[rt % 2][:],
                        XcT[par][:, kt * 1024 + rt * 128:
                                 kt * 1024 + (rt + 1) * 128],
                        Wv_b[:, kt * 128:(kt + 1) * 128],
                        start=(kt == 0), stop=(kt == 7),
                        skip_group_check=True))
                return i
            p.add("tensor", mmv, reads=[xtb, R.buf("Wv")], writes=[pzb])
            p.add("vector", lambda v, rt=rt, par=par: v.tensor_tensor(
                out=V_f32[:, rt * 128:(rt + 1) * 128], in0=pZ[rt % 2][:],
                in1=bvbc[:], op=ALU.add),
                reads=[pzb, R.buf("bvbc")], writes=[vb])
        for rt in range(8):
            pzb = Rp.buf(("pZ", rt % 2))
            def mmz(t, par=par, rt=rt):
                i = []
                for kt in range(8):
                    i.append(t.matmul(
                        pZ[rt % 2][:],
                        XcT[par][:, kt * 1024 + rt * 128:
                                 kt * 1024 + (rt + 1) * 128],
                        Wt_b[:, kt * 128:(kt + 1) * 128],
                        start=(kt == 0), stop=(kt == 7),
                        skip_group_check=True))
                return i
            p.add("tensor", mmz, reads=[xtb, R.buf("Wb")], writes=[pzb])
            p.add("vector", lambda v, rt=rt, par=par: v.tensor_tensor(
                out=R_f32[:, rt * 128:(rt + 1) * 128], in0=pZ[rt % 2][:],
                in1=V_f32[:, rt * 128:(rt + 1) * 128], op=ALU.subtract),
                reads=[pzb, vb], writes=[rb])

        eb = R.buf("E")
        yb = R.buf(("Y", par))
        for i in range(8):
            gb = R.buf(("G", i % 2))
            if i == 0:
                p.add("scalar", lambda s, i=i: s.copy(
                    out=G_b[i % 2][:], in_=R_f32[:, 0:128]),
                    reads=[rb], writes=[gb])
            else:
                pxb = Rp.buf("pX")
                def mmx(t, par=par, i=i):
                    ii = []
                    for jj in range(i):
                        ii.append(t.matmul(
                            pX[:],
                            B_sb[par][:, jj * 1024 + i * 128:
                                     jj * 1024 + (i + 1) * 128],
                            E_b[:, jj * 128:(jj + 1) * 128],
                            start=(jj == 0), stop=(jj == i - 1),
                            skip_group_check=True))
                    return ii
                p.add("tensor", mmx, reads=[bb, eb], writes=[pxb])
                p.add("vector", lambda v, i=i: v.tensor_tensor(
                    out=G_b[i % 2][:], in0=pX[:],
                    in1=R_f32[:, i * 128:(i + 1) * 128], op=ALU.add),
                    reads=[pxb, rb], writes=[gb])
            peb = Rp.buf("pE")
            p.add("tensor", lambda t, par=par, i=i: t.matmul(
                pE[:], M_sb[par][:, i * 128:(i + 1) * 128], G_b[i % 2][:],
                start=True, stop=True, skip_group_check=True),
                reads=[mb, gb], writes=[peb])
            p.add("vector", lambda v, i=i: v.tensor_copy(
                out=E_b[:, i * 128:(i + 1) * 128], in_=pE[:]),
                reads=[peb], writes=[eb])
            p.add("vector", lambda v, i=i, par=par: v.tensor_tensor(
                out=Y_f32[par][:, i * 128:(i + 1) * 128], in0=pE[:],
                in1=V_f32[:, i * 128:(i + 1) * 128], op=ALU.add),
                reads=[peb, vb], writes=[yb])

        for dt_ in range(8):
            pwb = Rp.buf(("pW", dt_ % 2))
            def mmw(t, par=par, dt_=dt_):
                i = []
                for rt in range(8):
                    i.append(t.matmul(
                        pW[dt_ % 2][:],
                        Xcr[par][:, rt * 1024 + dt_ * 128:
                                 rt * 1024 + (dt_ + 1) * 128],
                        E_b[:, rt * 128:(rt + 1) * 128],
                        start=(rt == 0), stop=(rt == 7),
                        skip_group_check=True))
                return i
            p.add("tensor", mmw, reads=[xrb2, eb], writes=[pwb])
            p.add("vector", lambda v, dt_=dt_: v.scalar_tensor_tensor(
                out=W_f32[:, dt_ * 128:(dt_ + 1) * 128], in0=pW[dt_ % 2][:],
                scalar=mcvec[:], in1=W_f32[:, dt_ * 128:(dt_ + 1) * 128],
                op0=ALU.mult, op1=ALU.add),
                reads=[pwb, R.buf("mc")], writes=[R.buf("W")])
            p.add("scalar", lambda s, dt_=dt_: s.copy(
                out=Wt_b[:, dt_ * 128:(dt_ + 1) * 128],
                in_=W_f32[:, dt_ * 128:(dt_ + 1) * 128]),
                reads=[R.buf("W")], writes=[R.buf("Wb")])

        stc = p.add("sync", lambda e, c=c, par=par: e.dma_start(
            out=y_out[c].rearrange("(rt p) e -> p rt e", p=128),
            in_=Y_f32[par][:]), reads=[yb], dma=True)
        store_steps.append(stc)

    p.add("gpsimd", lambda e: None, deps=store_steps)
    return ap, p, st


def _prep_l2b(x2_full, Ball, Mall, inputs, k):
    xcT = np.stack([
        np.ascontiguousarray(
            x2_full[:, c * 128:(c + 1) * 128, :].transpose(2, 1, 0)
            .reshape(1024, 1024)) for c in range(NCHUNK)])
    xcr = np.stack([
        np.ascontiguousarray(
            x2_full[:, c * 128:(c + 1) * 128, :].transpose(1, 0, 2)
            .reshape(1024, 1024)) for c in range(NCHUNK)])
    W0 = np.asarray(inputs["L1_attn_W0"])
    Wkv = np.asarray(inputs["L1_attn_Wkv"])
    bkv = np.asarray(inputs["L1_attn_bkv"])
    sl = slice(k * ESL, (k + 1) * ESL)
    return {
        "xcT": xcT.astype(np.float32), "xcr": xcr.astype(np.float32),
        "Ball": Ball, "Mall": Mall,
        "w0t": np.ascontiguousarray(W0[sl, :].T),
        "wvt": np.ascontiguousarray(Wkv[1024:, :][sl, :].T),
        "bvr": np.ascontiguousarray(bkv[1024:][sl].reshape(1, ESL)),
        "eta": np.asarray(inputs["L1_attn_eta"], np.float32).reshape(1, 1),
        "onesf": np.ones((1, 128), np.float32),
    }


def build_l3(nc):
    ap = {}
    for name, shape in [
        ("x2b", [S, D]), ("yb", [S, D]), ("rwt", [D, NE]), ("rbr", [1, NE]),
        ("wgt", [NE, D, NI]), ("wut", [NE, D, NI]), ("wdt", [NE, NI, D]),
        ("identf", [128, 128]), ("onesf", [1, 128]), ("epsc", [128, 1]),
    ]:
        ap[name] = nc.dram_tensor(name, shape, F32, kind="ExternalInput").ap()
    x4_out = nc.dram_tensor("x4", [S, D], F32, kind="ExternalOutput").ap()
    x3d = nc.dram_tensor("x3d", [S, D], F32, kind="Internal").ap()

    st = ExitStack()
    E = st.enter_context
    x3T = E(nc.sbuf_tensor("x3T", [128, 16384], BF16))
    y_acc = E(nc.sbuf_tensor("y_acc", [128, 16384], F32))
    wg_sb = [E(nc.sbuf_tensor(f"wg{i}", [128, 4096], BF16)) for i in range(2)]
    wu_sb = [E(nc.sbuf_tensor(f"wu{i}", [128, 4096], BF16)) for i in range(2)]
    wd_sb = [E(nc.sbuf_tensor(f"wd{i}", [128, 4096], BF16)) for i in range(2)]
    hT_sb = [E(nc.sbuf_tensor(f"hT{i}", [128, 2048], BF16)) for i in range(2)]
    sil_t = [E(nc.sbuf_tensor(f"sil{i}", [128, 512], BF16)) for i in range(2)]
    sco = E(nc.sbuf_tensor("sco", [128, 256], F32))
    comb = E(nc.sbuf_tensor("comb", [128, 256], F32))
    rts = E(nc.sbuf_tensor("rts", [128, 1152], F32))
    rw_sb = E(nc.sbuf_tensor("rw_sb", [128, 128], F32))
    rwb_sb = E(nc.sbuf_tensor("rwb_sb", [128, 128], BF16))
    rbbc = E(nc.sbuf_tensor("rbbc", [128, 16], F32))
    xb_t = [E(nc.sbuf_tensor(f"xb{i}", [128, 1024], BF16)) for i in range(2)]
    f32_t = [E(nc.sbuf_tensor(f"f32_{i}", [128, 1024], F32)) for i in range(6)]
    stat = E(nc.sbuf_tensor("stat", [128, 16], F32))
    identb = E(nc.sbuf_tensor("identb", [128, 128], BF16))
    ones_sb = E(nc.sbuf_tensor("ones_sb", [1, 128], F32))
    eps_sb = E(nc.sbuf_tensor("eps_sb", [128, 1], F32))
    rb_sb = E(nc.sbuf_tensor("rb_sb", [1, 16], F32))

    pG = [E(nc.psum_tensor(f"pG{i}", [128, 512], F32)) for i in range(2)]
    pU = [E(nc.psum_tensor(f"pU{i}", [128, 512], F32)) for i in range(2)]
    pY = [E(nc.psum_tensor(f"pY{i}", [128, 512], F32)) for i in range(2)]
    pR = E(nc.psum_tensor("pR", [128, 16], F32))
    pC = E(nc.psum_tensor("pC", [128, 128], BF16))

    p = Prog()
    R = Region()
    Rp = Region()
    Rd = Region()

    p.add("sync", lambda e: e.dma_start(out=ones_sb[:], in_=ap["onesf"]),
          writes=[R.buf("ones")], dma=True)
    p.add("sync", lambda e: e.dma_start(out=eps_sb[:], in_=ap["epsc"]),
          writes=[R.buf("eps")], dma=True)
    p.add("sync", lambda e: e.dma_start(out=rb_sb[:], in_=ap["rbr"]),
          writes=[R.buf("rb")], dma=True)
    p.add("sync", lambda e: e.dma_start(out=rw_sb[:], in_=ap["identf"]),
          writes=[R.buf("idf")], dma=True)
    p.add("scalar", lambda s: s.copy(out=identb[:], in_=rw_sb[:]),
          reads=[R.buf("idf")], writes=[R.buf("identb")])
    p.add("gpsimd", lambda g: g.dma_start(
        out=rwb_sb[:], in_=ap["rwt"].rearrange("(kt p) e -> p kt e", p=128)),
        writes=[R.buf("rw")], dma=True)
    p.add("tensor", lambda t: t.matmul(pR[:], ones_sb[:], rb_sb[:],
                                       start=True, stop=True,
                                       skip_group_check=True),
          reads=[R.buf("ones"), R.buf("rb")], writes=[Rp.buf("pR")])
    p.add("scalar", lambda sc: sc.copy(out=rbbc[:], in_=pR[:]),
          reads=[Rp.buf("pR")], writes=[R.buf("rbbc")])

    for tt in range(16):
        xrb = R.buf(("xr", tt % 2))
        yrb = R.buf(("yr", tt % 2))
        p.add("sync", lambda e, tt=tt: e.dma_start(
            out=f32_t[tt % 2][:], in_=ap["x2b"][tt * 128:(tt + 1) * 128, :]),
            writes=[xrb], dma=True)
        p.add("sync", lambda e, tt=tt: e.dma_start(
            out=f32_t[2 + tt % 2][:], in_=ap["yb"][tt * 128:(tt + 1) * 128, :]),
            writes=[yrb], dma=True)
        sb = xrb
        p.add("vector", lambda v, tt=tt: v.tensor_tensor(
            out=f32_t[tt % 2][:], in0=f32_t[tt % 2][:],
            in1=f32_t[2 + tt % 2][:], op=ALU.add),
            reads=[yrb], writes=[sb], fence=True)
        outb = yrb
        _ln_emit(p, eps_sb[:], f32_t[tt % 2][:], sb,
                 f32_t[2 + tt % 2][:], outb,
                 f32_t[4][:], R.buf("sq"),
                 stat[:, (tt % 2) * 8:(tt % 2) * 8 + 8], R.buf(("st", tt % 2)),
                 extra_reads=[R.buf("eps")])
        p.add("sync", lambda e, tt=tt: e.dma_start(
            out=x3d[tt * 128:(tt + 1) * 128, :], in_=f32_t[2 + tt % 2][:]),
            reads=[outb], writes=[Rd.buf(tt)], dma=True)
        xbb = R.buf(("xb", tt % 2))
        p.add("scalar", lambda s, tt=tt: s.copy(
            out=xb_t[tt % 2][:], in_=f32_t[2 + tt % 2][:]),
            reads=[outb], writes=[xbb])
        for dt_ in range(8):
            pcb = Rp.buf("pC")
            p.add("tensor", lambda t, tt=tt, dt_=dt_: t.transpose(
                pC[:], xb_t[tt % 2][:, dt_ * 128:(dt_ + 1) * 128], identb[:]),
                reads=[xbb, R.buf("identb")], writes=[pcb])
            p.add("vector", lambda v, tt=tt, dt_=dt_: v.tensor_copy(
                out=x3T[:, dt_ * 2048 + tt * 128:dt_ * 2048 + (tt + 1) * 128],
                in_=pC[:]), reads=[pcb], writes=[R.buf("x3T")])
        prb = Rp.buf("pR")
        def mmr(t, tt=tt):
            i = []
            for kt in range(8):
                i.append(t.matmul(
                    pR[:], x3T[:, kt * 2048 + tt * 128:kt * 2048 + (tt + 1) * 128],
                    rwb_sb[:, kt * 16:(kt + 1) * 16],
                    start=(kt == 0), stop=(kt == 7), skip_group_check=True))
            return i
        p.add("tensor", mmr, reads=[R.buf("x3T"), R.buf("rw")], writes=[prb])
        p.add("scalar", lambda s, tt=tt: s.activation(
            out=sco[:, tt * 16:(tt + 1) * 16], in_=pR[:], func=AF.Sigmoid),
            reads=[prb], writes=[R.buf("sco")])

    scob = R.buf("sco")
    rtb = R.buf("rts")
    cmb = R.buf("comb")
    SCH, GSC, PW, THR, GMK, OFF, MK, MX, EQ = (0, 256, 320, 384, 448, 512,
                                               576, 832, 896)

    def vop(fn, reads, writes):
        p.add("vector", fn, reads=reads, writes=writes, fence=True)

    sch = rts[:, SCH:SCH + 256]
    gsc = rts[:, GSC:GSC + 64]
    vop(lambda v: v.tensor_tensor(
        out=sch.rearrange("p (t e) -> p e t", e=16),
        in0=sco[:].rearrange("p (t e) -> p e t", e=16),
        in1=rbbc[:].broadcast_to((128, 16, 16)),
        op=ALU.add), [scob, R.buf("rbbc")], [rtb])
    a_, b_, c_, d_ = (sch[:, 0::4], sch[:, 1::4], sch[:, 2::4], sch[:, 3::4])
    vop(lambda v: v.tensor_tensor(out=rts[:, PW:PW + 64], in0=a_, in1=b_,
                                  op=ALU.max), [rtb], [rtb])
    vop(lambda v: v.tensor_tensor(out=rts[:, THR:THR + 64], in0=c_, in1=d_,
                                  op=ALU.max), [rtb], [rtb])
    vop(lambda v: v.tensor_tensor(out=gsc, in0=rts[:, PW:PW + 64],
                                  in1=rts[:, THR:THR + 64], op=ALU.add),
        [rtb], [rtb])
    vop(lambda v: v.tensor_tensor(out=rts[:, PW:PW + 64], in0=a_, in1=b_,
                                  op=ALU.add), [rtb], [rtb])
    vop(lambda v: v.tensor_tensor(out=rts[:, THR:THR + 64], in0=c_, in1=d_,
                                  op=ALU.add), [rtb], [rtb])
    vop(lambda v: v.tensor_tensor(out=rts[:, PW:PW + 64],
                                  in0=rts[:, PW:PW + 64],
                                  in1=rts[:, THR:THR + 64], op=ALU.max),
        [rtb], [rtb])
    vop(lambda v: v.tensor_tensor(out=gsc, in0=gsc, in1=rts[:, PW:PW + 64],
                                  op=ALU.max), [rtb], [rtb])
    g0, g1 = gsc[:, 0::2], gsc[:, 1::2]
    vop(lambda v: v.tensor_tensor(out=rts[:, PW:PW + 32], in0=g0, in1=g1,
                                  op=ALU.max), [rtb], [rtb])
    vop(lambda v: v.tensor_tensor(out=rts[:, PW + 32:PW + 64], in0=g0, in1=g1,
                                  op=ALU.min), [rtb], [rtb])
    gM0, gM1 = rts[:, PW:PW + 32][:, 0::2], rts[:, PW:PW + 32][:, 1::2]
    gm0, gm1 = (rts[:, PW + 32:PW + 64][:, 0::2],
                rts[:, PW + 32:PW + 64][:, 1::2])
    vop(lambda v: v.tensor_tensor(out=rts[:, THR:THR + 16], in0=gM0, in1=gM1,
                                  op=ALU.min), [rtb], [rtb])
    vop(lambda v: v.tensor_tensor(out=rts[:, THR + 16:THR + 32], in0=gm0,
                                  in1=gm1, op=ALU.max), [rtb], [rtb])
    vop(lambda v: v.tensor_tensor(out=rts[:, THR:THR + 16],
                                  in0=rts[:, THR:THR + 16],
                                  in1=rts[:, THR + 16:THR + 32], op=ALU.max),
        [rtb], [rtb])
    vop(lambda v: v.tensor_tensor(
        out=rts[:, GMK:GMK + 64],
        in0=gsc.rearrange("p (t g) -> p t g", g=4),
        in1=rts[:, THR:THR + 16].broadcast_to((128, 16, 4)), op=ALU.is_ge), [rtb], [rtb])
    vop(lambda v: v.tensor_scalar(
        out=rts[:, OFF:OFF + 64], in0=rts[:, GMK:GMK + 64], scalar1=1e9,
        scalar2=-1e9, op0=ALU.mult, op1=ALU.add), [rtb], [rtb])
    vop(lambda v: v.tensor_tensor(
        out=rts[:, MK:MK + 256], in0=sch.rearrange("p (q e) -> p q e", e=4),
        in1=rts[:, GMK:GMK + 64].broadcast_to((128, 64, 4)), op=ALU.mult), [rtb], [rtb])
    vop(lambda v: v.tensor_tensor(
        out=rts[:, MK:MK + 256],
        in0=rts[:, MK:MK + 256].rearrange("p (q e) -> p q e", e=4),
        in1=rts[:, OFF:OFF + 64].broadcast_to((128, 64, 4)), op=ALU.add), [rtb], [rtb])
    vop(lambda v: v.tensor_copy(out=comb[:], in_=rts[:, MK:MK + 256]),
        [rtb], [cmb])
    for r_ in range(4):
        vop(lambda v, r_=r_: v.tensor_reduce(
            out=rts[:, MX + r_ * 16:MX + (r_ + 1) * 16],
            in_=rts[:, MK:MK + 256].rearrange("p (t e) -> p t e", e=16),
            axis=mybir.AxisListType.X, op=ALU.max), [rtb], [rtb])
        if r_ < 3:
            vop(lambda v, r_=r_: v.tensor_tensor(
                out=rts[:, EQ:EQ + 256],
                in0=rts[:, MK:MK + 256].rearrange("p (t e) -> p t e", e=16),
                in1=rts[:, MX + r_ * 16:MX + (r_ + 1) * 16].broadcast_to(
                    (128, 16, 16)), op=ALU.is_equal), [rtb], [rtb])
            vop(lambda v: v.scalar_tensor_tensor(
                out=rts[:, MK:MK + 256], in0=rts[:, EQ:EQ + 256],
                scalar=-2e9, in1=rts[:, MK:MK + 256], op0=ALU.mult,
                op1=ALU.add), [rtb], [rtb])
    vop(lambda v: v.tensor_tensor(
        out=rts[:, EQ:EQ + 256],
        in0=comb[:].rearrange("p (t e) -> p t e", e=16),
        in1=rts[:, MX + 48:MX + 64].broadcast_to((128, 16, 16)), op=ALU.is_ge), [rtb, cmb], [rtb])
    vop(lambda v: v.tensor_tensor(out=rts[:, MK:MK + 256],
                                  in0=rts[:, EQ:EQ + 256], in1=sco[:],
                                  op=ALU.mult), [rtb, scob], [rtb])
    vop(lambda v: v.tensor_reduce(
        out=rts[:, THR:THR + 16],
        in_=rts[:, MK:MK + 256].rearrange("p (t e) -> p t e", e=16),
        axis=mybir.AxisListType.X, op=ALU.add), [rtb], [rtb])
    vop(lambda v: v.tensor_scalar(
        out=rts[:, THR + 16:THR + 32], in0=rts[:, THR:THR + 16],
        scalar1=1e-20, scalar2=None, op0=ALU.add), [rtb], [rtb])
    vop(lambda v: v.reciprocal(out=rts[:, THR + 32:THR + 48],
                               in_=rts[:, THR + 16:THR + 32]), [rtb], [rtb])
    vop(lambda v: v.tensor_tensor(
        out=comb[:], in0=rts[:, MK:MK + 256].rearrange("p (t e) -> p t e", e=16),
        in1=rts[:, THR + 32:THR + 48].broadcast_to((128, 16, 16)), op=ALU.mult), [rtb], [cmb])
    vop(lambda v: v.tensor_scalar(
        out=comb[:], in0=comb[:], scalar1=2.5, scalar2=None, op0=ALU.mult),
        [cmb], [cmb])

    yab = R.buf("y_acc")
    for e_ in range(NE):
        par = e_ % 2
        wgb, wub, wdb = (R.buf(("wg", par)), R.buf(("wu", par)),
                         R.buf(("wd", par)))
        p.add("gpsimd", lambda g, e_=e_, par=par: g.dma_start(
            out=wg_sb[par][:],
            in_=ap["wgt"][e_].rearrange("(kt p) i -> p kt i", p=128)),
            writes=[wgb], dma=True)
        p.add("gpsimd", lambda g, e_=e_, par=par: g.dma_start(
            out=wu_sb[par][:],
            in_=ap["wut"][e_].rearrange("(kt p) i -> p kt i", p=128)),
            writes=[wub], dma=True)
        p.add("gpsimd", lambda g, e_=e_, par=par: g.dma_start(
            out=wd_sb[par][:],
            in_=ap["wdt"][e_].rearrange("(it p) d -> p it d", p=128)),
            writes=[wdb], dma=True)
        for tc in range(4):
            hb = R.buf(("hT", tc % 2))
            for it in range(4):
                pgb, pub = Rp.buf(("pG", it % 2)), Rp.buf(("pU", it % 2))
                def mk_mm(w, pp, par=par, tc=tc, it=it):
                    def mm(t):
                        i = []
                        for kt in range(8):
                            i.append(t.matmul(
                                pp[it % 2][:],
                                w[par][:, kt * 512 + it * 128:
                                       kt * 512 + (it + 1) * 128],
                                x3T[:, kt * 2048 + tc * 512:
                                    kt * 2048 + (tc + 1) * 512],
                                start=(kt == 0), stop=(kt == 7),
                                skip_group_check=True))
                        return i
                    return mm
                p.add("tensor", mk_mm(wg_sb, pG), reads=[wgb, R.buf("x3T")],
                      writes=[pgb])
                p.add("tensor", mk_mm(wu_sb, pU), reads=[wub, R.buf("x3T")],
                      writes=[pub])
                slb = R.buf(("sil", it % 2))
                p.add("scalar", lambda s, it=it: s.activation(
                    out=sil_t[it % 2][:], in_=pG[it % 2][:], func=AF.Silu),
                    reads=[pgb], writes=[slb])
                p.add("vector", lambda v, it=it, tc=tc: v.tensor_tensor(
                    out=hT_sb[tc % 2][:, it * 512:(it + 1) * 512],
                    in0=sil_t[it % 2][:], in1=pU[it % 2][:], op=ALU.mult),
                    reads=[slb, pub], writes=[hb])
            for tw in range(4):
                tt = tc * 4 + tw
                for half in range(2):
                    pyb = Rp.buf(("pY", half))
                    def mmd(t, par=par, tc=tc, tw=tw, half=half):
                        i = []
                        for it in range(4):
                            i.append(t.matmul(
                                pY[half][:],
                                hT_sb[tc % 2][:, it * 512 + tw * 128:
                                              it * 512 + (tw + 1) * 128],
                                wd_sb[par][:, it * 1024 + half * 512:
                                           it * 1024 + (half + 1) * 512],
                                start=(it == 0), stop=(it == 3),
                                skip_group_check=True))
                        return i
                    p.add("tensor", mmd, reads=[hb, wdb], writes=[pyb])
                    if e_ == 0:
                        p.add("vector", lambda v, tt=tt, half=half, e_=e_:
                              v.tensor_scalar(
                                  out=y_acc[:, tt * 1024 + half * 512:
                                            tt * 1024 + (half + 1) * 512],
                                  in0=pY[half][:],
                                  scalar1=comb[:, tt * 16 + e_:tt * 16 + e_ + 1],
                                  scalar2=None, op0=ALU.mult),
                              reads=[pyb, cmb], writes=[yab])
                    else:
                        p.add("vector", lambda v, tt=tt, half=half, e_=e_:
                              v.scalar_tensor_tensor(
                                  out=y_acc[:, tt * 1024 + half * 512:
                                            tt * 1024 + (half + 1) * 512],
                                  in0=pY[half][:],
                                  scalar=comb[:, tt * 16 + e_:tt * 16 + e_ + 1],
                                  in1=y_acc[:, tt * 1024 + half * 512:
                                            tt * 1024 + (half + 1) * 512],
                                  op0=ALU.mult, op1=ALU.add),
                              reads=[pyb, cmb], writes=[yab])

    store_steps = []
    for tt in range(16):
        xrb = R.buf(("fx", tt % 2))
        p.add("sync", lambda e, tt=tt: e.dma_start(
            out=f32_t[tt % 2][:], in_=x3d[tt * 128:(tt + 1) * 128, :]),
            reads=[Rd.buf(tt)], writes=[xrb], dma=True)
        sb = xrb
        p.add("vector", lambda v, tt=tt: v.tensor_tensor(
            out=f32_t[tt % 2][:], in0=f32_t[tt % 2][:],
            in1=y_acc[:, tt * 1024:(tt + 1) * 1024], op=ALU.add),
            reads=[yab], writes=[sb], fence=True)
        outb = R.buf(("fo", tt % 2))
        _ln_emit(p, eps_sb[:], f32_t[tt % 2][:], sb,
                 f32_t[2 + tt % 2][:], outb,
                 f32_t[4][:], R.buf("sq"),
                 stat[:, (tt % 2) * 8:(tt % 2) * 8 + 8], R.buf(("st", tt % 2)),
                 extra_reads=[R.buf("eps")])
        stc = p.add("sync", lambda e, tt=tt: e.dma_start(
            out=x4_out[tt * 128:(tt + 1) * 128, :], in_=f32_t[2 + tt % 2][:]),
            reads=[outb], dma=True)
        store_steps.append(stc)
    p.add("gpsimd", lambda e: None, deps=store_steps)
    return ap, p, st


def _prep_l3(x2_full, y_full, inputs, b):
    return {
        "x2b": np.ascontiguousarray(x2_full[b]).astype(np.float32),
        "yb": np.ascontiguousarray(y_full[b]).astype(np.float32),
        "rwt": np.ascontiguousarray(
            np.asarray(inputs["L1_ffn_router_weight"]).T).astype(np.float32),
        "rbr": np.asarray(inputs["L1_ffn_router_bias"]).reshape(1, NE)
            .astype(np.float32),
        "wgt": np.ascontiguousarray(
            np.asarray(inputs["L1_ffn_w_gate"]).transpose(0, 2, 1))
            .astype(np.float32),
        "wut": np.ascontiguousarray(
            np.asarray(inputs["L1_ffn_w_up"]).transpose(0, 2, 1))
            .astype(np.float32),
        "wdt": np.ascontiguousarray(
            np.asarray(inputs["L1_ffn_w_down"]).transpose(0, 2, 1))
            .astype(np.float32),
        "identf": np.eye(128, dtype=np.float32),
        "onesf": np.ones((1, 128), np.float32),
        "epsc": np.full((128, 1), 1e-5, dtype=np.float32),
    }


def _run(build, in_maps):
    nc = bass.Bass("TRN2", target_bir_lowering=False)
    out = build(nc)
    p, st = out[-2], out[-1]
    with st:
        p.emit(nc, st)
    return run_bass_kernel_spmd(nc, in_maps, core_ids=list(range(NCORE)))


def kernel(**inputs):
    inputs = {k: np.asarray(v) for k, v in inputs.items()}

    res0 = _run(build_l0, [_prep_l0(inputs, b) for b in range(NCORE)])
    x2_full = np.stack([np.asarray(res0.results[b]["x2"])
                        for b in range(NCORE)])

    res2a = _run(build_l2a, [_prep_l2a(x2_full, inputs, k)
                             for k in range(NCORE)])
    Ball = np.concatenate([np.asarray(res2a.results[k]["Bout"])
                           for k in range(NCORE)])
    Mall = np.concatenate([np.asarray(res2a.results[k]["Mout"])
                           for k in range(NCORE)])

    res2b = _run(build_l2b, [_prep_l2b(x2_full, Ball, Mall, inputs, k)
                             for k in range(NCORE)])
    y_full = np.zeros((B, S, D), np.float32)
    for k in range(NCORE):
        yk = np.asarray(res2b.results[k]["yout"]).reshape(NCHUNK, CH, B, ESL)
        y_full[:, :, k * ESL:(k + 1) * ESL] = \
            yk.transpose(2, 0, 1, 3).reshape(B, S, ESL)

    res3 = _run(build_l3, [_prep_l3(x2_full, y_full, inputs, b)
                           for b in range(NCORE)])
    x4 = np.stack([np.asarray(res3.results[b]["x4"]) for b in range(NCORE)])
    return x4.astype(np.float32)


# version 05
# speedup vs baseline: 1.0117x; 1.0117x over previous
import numpy as np
import ml_dtypes
from contextlib import ExitStack
import concourse.bass as bass
import concourse.mybir as mybir
from concourse.bass_utils import run_bass_kernel_spmd

F32 = mybir.dt.float32
BF16 = mybir.dt.bfloat16
AF = mybir.ActivationFunctionType
ALU = mybir.AluOpType
BF = ml_dtypes.bfloat16

B, S, D = 8, 2048, 1024
H, HN = 16, 64
RANK, DFF = 256, 4096
NE, NI = 16, 512
NCORE = 8
ESL = D // NCORE
CH = 128
NCHUNK = S // CH
LN_EPS = 1e-5


class _Step:
    __slots__ = ("eng", "fn", "deps", "dma", "idx", "need_inc", "sem", "val", "g")

    def __init__(self, eng, fn, deps, dma, idx):
        self.eng, self.fn, self.deps, self.dma, self.idx = eng, fn, deps, dma, idx
        self.need_inc = False
        self.sem = None
        self.val = 0


class Buf:
    __slots__ = ("last_write", "last_reads")

    def __init__(self):
        self.last_write = None
        self.last_reads = []


class Prog:
    ENGS = ("sync", "scalar", "vector", "tensor", "gpsimd")
    DMA_POOL = 12

    def __init__(self):
        self.steps = {e: [] for e in self.ENGS}
        self.dma_count = {e: 0 for e in self.ENGS}
        self.gidx = 0
        self.last_dma = {}

    def add(self, eng, fn, reads=(), writes=(), deps=(), dma=False,
            fence=False):
        deps = list(deps)
        for b in reads:
            if b.last_write is not None:
                deps.append(b.last_write)
        for b in writes:
            if b.last_write is not None:
                deps.append(b.last_write)
            deps.extend(b.last_reads)
        if fence:
            same = [d for d in deps if d.eng == eng and not d.dma]
            if same:
                hop_eng = "sync" if eng != "sync" else "gpsimd"
                hop = self.add(hop_eng, lambda e: e.nop(), deps=same)
                deps = [d for d in deps
                        if d.dma or d.eng != eng] + [hop]
        st = _Step(eng, fn, deps, dma, len(self.steps[eng]))
        st.g = self.gidx
        self.gidx += 1
        self.steps[eng].append(st)
        for b in reads:
            b.last_reads.append(st)
        for b in writes:
            b.last_write = st
            b.last_reads = []
        if dma:
            st.sem = ("q", eng, self.dma_count[eng] % self.DMA_POOL)
            self.dma_count[eng] += 1
            prev = self.last_dma.get(st.sem)
            if prev is not None:
                st.deps.append(prev)
            self.last_dma[st.sem] = st
        else:
            st.sem = ("c", eng)
        return st

    def emit(self, nc, stack):
        for e in self.ENGS:
            for st in self.steps[e]:
                for d in st.deps:
                    if d.eng != st.eng or d.dma:
                        d.need_inc = True
                if st.dma:
                    st.need_inc = True
        cnt, sem_keys = {}, set()
        sem_hist = {}
        for e in self.ENGS:
            for st in sorted(self.steps[e], key=lambda x: x.g):
                if st.need_inc:
                    sem_keys.add(st.sem)
                    cnt[st.sem] = cnt.get(st.sem, 0) + (16 if st.dma else 1)
                    st.val = cnt[st.sem]
                    if st.dma:
                        sem_hist.setdefault(st.sem, []).append((st.g, st.val))

        import bisect

        def wait_val(dep, consumer):
            return dep.val
        sems = {k: stack.enter_context(
            nc.semaphore("s_" + "_".join(map(str, k)))) for k in sem_keys}
        block = stack.enter_context(nc.Block())
        prog = self

        def stream(eng_name):
            def run(eng):
                waited = {}
                for st in prog.steps[eng_name]:
                    for d in st.deps:
                        if d.eng == st.eng and not d.dma:
                            continue
                        v = wait_val(d, st)
                        if waited.get(d.sem, -1) >= v:
                            continue
                        eng.wait_ge(sems[d.sem], v)
                        waited[d.sem] = v
                    insts = st.fn(eng)
                    if st.need_inc:
                        inst = insts[-1] if isinstance(insts, (list, tuple)) \
                            else insts
                        assert inst is not None, f"step on {eng_name} needs inc"
                        inst.then_inc(sems[st.sem], 16 if st.dma else 1)
            return run

        for e in self.ENGS:
            getattr(block, e)(stream(e))


class Region:

    def __init__(self):
        self.bufs = {}

    def buf(self, key):
        return self.bufs.setdefault(key, Buf())

    def retire(self):
        steps = []
        for b in self.bufs.values():
            if b.last_write is not None:
                steps.append(b.last_write)
            steps.extend(b.last_reads)
        self.bufs = {}
        return steps


def _ln_emit(p, epsc_ap, s_ap, s_buf, out_ap, out_buf, sq_ap, sq_buf,
             stat_ap, stat_buf, extra_reads=()):
    D_ = 1024
    p.add("vector", lambda v: v.tensor_reduce(
        out=stat_ap[:, 0:1], in_=s_ap, axis=mybir.AxisListType.X, op=ALU.add),
        reads=[s_buf] + list(extra_reads), writes=[stat_buf], fence=True)
    p.add("scalar", lambda sc: sc.activation(
        out=sq_ap, in_=s_ap, func=AF.Square, accum_out=stat_ap[:, 2:3]),
        reads=[s_buf], writes=[sq_buf, stat_buf])
    p.add("scalar", lambda sc: sc.mul(
        out=stat_ap[:, 1:2], in_=stat_ap[:, 0:1], mul=1.0 / D_),
        reads=[stat_buf], writes=[stat_buf], fence=True)
    p.add("vector", lambda v: v.tensor_scalar(
        out=stat_ap[:, 3:4], in0=stat_ap[:, 2:3], scalar1=1.0 / D_,
        scalar2=None, op0=ALU.mult),
        reads=[stat_buf], writes=[stat_buf], fence=True)
    p.add("vector", lambda v: v.tensor_tensor(
        out=stat_ap[:, 4:5], in0=stat_ap[:, 1:2], in1=stat_ap[:, 1:2],
        op=ALU.mult), reads=[stat_buf], writes=[stat_buf], fence=True)
    p.add("gpsimd", lambda g: g.tensor_tensor(
        out=stat_ap[:, 5:6], in0=stat_ap[:, 3:4], in1=stat_ap[:, 4:5],
        op=ALU.subtract), reads=[stat_buf], writes=[stat_buf])
    p.add("scalar", lambda sc: sc.activation(
        out=stat_ap[:, 6:7], in_=stat_ap[:, 5:6], func=AF.Sqrt,
        bias=epsc_ap), reads=[stat_buf], writes=[stat_buf])
    p.add("vector", lambda v: v.reciprocal(
        out=stat_ap[:, 7:8], in_=stat_ap[:, 6:7]),
        reads=[stat_buf], writes=[stat_buf], fence=True)
    p.add("vector", lambda v: v.tensor_scalar(
        out=out_ap, in0=s_ap, scalar1=stat_ap[:, 1:2],
        scalar2=stat_ap[:, 7:8], op0=ALU.subtract, op1=ALU.mult),
        reads=[s_buf, stat_buf], writes=[out_buf], fence=True)


def build_l0(nc):
    ap = {}
    for name, shape in [
        ("x", [S, D]), ("wqk", [8, 128, 2048]), ("wv", [8, 128, 1024]),
        ("wo", [D, D]), ("u1", [D, RANK]), ("v1", [RANK, DFF]),
        ("b1t", [128, 32]), ("u2", [DFF, RANK]), ("v2", [RANK, D]),
        ("b2r", [1, D]), ("identf", [128, 128]), ("onesf", [1, 128]),
        ("selbcf", [65, 64]), ("epsc", [128, 1]),
    ]:
        ap[name] = nc.dram_tensor(name, shape, F32, kind="ExternalInput").ap()
    x2_out = nc.dram_tensor("x2", [S, D], F32, kind="ExternalOutput").ap()
    import os
    dbg = {}
    if os.environ.get("K_DEBUG"):
        for n, shp, dt in [("d_qk", [128, 4096], BF16), ("d_v", [128, 16640], BF16),
                           ("d_ct", [128, 16384], BF16), ("d_av", [65, 2048], F32),
                           ("d_xT", [128, 16384], BF16), ("d_s", [128, 1024], F32),
                           ("d_stat", [128, 16], F32)]:
            dbg[n] = nc.dram_tensor(n, shp, dt, kind="ExternalOutput").ap()

    st = ExitStack()
    E = st.enter_context
    BIGA = E(nc.sbuf_tensor("BIGA", [128, 16384], BF16))
    BIGB = E(nc.sbuf_tensor("BIGB", [128, 16384], BF16))
    BIGC = E(nc.sbuf_tensor("BIGC", [128, 16640], BF16))
    BIGD = E(nc.sbuf_tensor("BIGD", [128, 16384], BF16))
    WVO = E(nc.sbuf_tensor("WVO", [128, 8192], BF16))
    qk_b = E(nc.sbuf_tensor("qk_b", [128, 4096], BF16))
    stage_sb = [E(nc.sbuf_tensor(f"stage{i}", [64, 512], BF16)) for i in range(2)]
    selbc = E(nc.sbuf_tensor("selbc", [65, 64], F32))
    xb_t = [E(nc.sbuf_tensor(f"xb{i}", [128, 1024], BF16)) for i in range(2)]
    at_t = [E(nc.sbuf_tensor(f"at{i}", [128, 512], BF16)) for i in range(4)]
    av_sb = E(nc.sbuf_tensor("av_sb", [65, 2048], F32))
    h_t = [E(nc.sbuf_tensor(f"h{i}", [128, 512], BF16)) for i in range(2)]
    f32_t = [E(nc.sbuf_tensor(f"f32_{i}", [128, 1024], F32)) for i in range(6)]
    stat = E(nc.sbuf_tensor("stat", [128, 16], F32))
    identb = E(nc.sbuf_tensor("identb", [128, 128], BF16))
    onesb = E(nc.sbuf_tensor("onesb", [1, 128], F32))
    b1_sb = E(nc.sbuf_tensor("b1_sb", [128, 32], F32))
    b2bc = E(nc.sbuf_tensor("b2bc", [128, 1024], F32))
    eps_sb = E(nc.sbuf_tensor("eps_sb", [128, 1], F32))

    pA = [E(nc.psum_tensor(f"pA{i}", [128, 512], F32)) for i in range(4)]
    pB = [E(nc.psum_tensor(f"pB{i}", [128, 512], F32)) for i in range(2)]
    pC = [E(nc.psum_tensor(f"pC{i}", [128, 128], BF16)) for i in range(2)]

    p = Prog()
    RA, RB, RC, RD, RW = Region(), Region(), Region(), Region(), Region()
    Rp = Region()
    Rx2 = Region()
    Rf = Region()
    Rsm = Region()

    cidf = Rf.buf("sq")
    p.add("sync", lambda e: e.dma_start(out=onesb[:], in_=ap["onesf"]),
          writes=[Rsm.buf("onesb")], dma=True)
    ld_idf = p.add("sync", lambda e: e.dma_start(
        out=f32_t[4][:, 0:128], in_=ap["identf"]), writes=[cidf], dma=True)
    p.add("scalar", lambda s: s.copy(out=identb[:], in_=f32_t[4][:, 0:128]),
          reads=[cidf], writes=[Rsm.buf("identb")])
    p.add("sync", lambda e: e.dma_start(out=b1_sb[:], in_=ap["b1t"]),
          writes=[Rsm.buf("b1")], dma=True)
    p.add("sync", lambda e: e.dma_start(out=selbc[:], in_=ap["selbcf"]),
          writes=[Rsm.buf("selbc")], dma=True)
    p.add("sync", lambda e: e.dma_start(out=eps_sb[:], in_=ap["epsc"]),
          writes=[Rsm.buf("eps")], dma=True)
    p.add("sync", lambda e: e.dma_start(out=f32_t[5][0:1, :],
        in_=ap["b2r"]), writes=[Rsm.buf("b2r")], dma=True)
    for j in range(2):
        def mmb(t, j=j):
            return t.matmul(pB[j][:], onesb[:],
                            f32_t[5][0:1, j * 512:(j + 1) * 512],
                            start=True, stop=True, skip_group_check=True)
        mb = p.add("tensor", mmb, reads=[Rsm.buf("b2r"), Rsm.buf("onesb")],
                   writes=[Rp.buf(("pB", j))])
        p.add("scalar", lambda sc, j=j: sc.copy(
            out=b2bc[:, j * 512:(j + 1) * 512], in_=pB[j][:]),
            reads=[Rp.buf(("pB", j))], writes=[Rsm.buf("b2bc")])

    for dt in range(8):
        p.add("gpsimd", lambda g, dt=dt: g.dma_start(
            out=BIGA[:, dt * 2048:(dt + 1) * 2048], in_=ap["wqk"][dt]),
            writes=[RA.buf(("wqk", dt))], dma=True)
        p.add("gpsimd", lambda g, dt=dt: g.dma_start(
            out=WVO[:, dt * 1024:(dt + 1) * 1024], in_=ap["wv"][dt]),
            writes=[RW.buf(("wv", dt))], dma=True)

    def transpose_to(p, src_ap, src_buf, dst_ap, dst_buf, n128, pc_i=[0]):
        for k in range(n128):
            j = pc_i[0] % 2
            pc_i[0] += 1
            pcb = Rp.buf(("pC", j))
            p.add("tensor", lambda t, k=k, j=j: t.transpose(
                pC[j][:], src_ap[:, k * 128:(k + 1) * 128], identb[:]),
                reads=[src_buf, Rsm.buf("identb")], writes=[pcb])
            p.add("vector", lambda v, k=k, j=j: v.tensor_copy(
                out=dst_ap(k), in_=pC[j][:]), reads=[pcb], writes=[dst_buf])

    for tt in range(16):
        xbb = RB.buf(("xb", tt % 2))
        p.add("gpsimd", lambda g, tt=tt: g.dma_start(
            out=xb_t[tt % 2][:], in_=ap["x"][tt * 128:(tt + 1) * 128, :]),
            writes=[xbb], dma=True)
        transpose_to(p, xb_t[tt % 2], xbb,
                     (lambda tt: lambda dt: BIGB[:, dt * 2048 + tt * 128:
                                                 dt * 2048 + (tt + 1) * 128])(tt),
                     RB.buf("xT"), 8)

    xT = lambda dt: BIGB[:, dt * 2048:(dt + 1) * 2048]
    xTb = RB.buf("xT")

    for tt in range(16):
        p.add("vector", lambda v, tt=tt: v.memset(
            BIGC[:, tt * 1040 + 64:(tt + 1) * 1040:65], 1.0),
            writes=[RC.buf(("vones", tt))])
    for tt in range(16):
        pv = [Rp.buf(("pA", 2 * (tt % 2))), Rp.buf(("pA", 2 * (tt % 2) + 1))]
        for half in range(2):
            def mmv(t, tt=tt, half=half):
                i = []
                for dt in range(8):
                    i.append(t.matmul(
                        pA[2 * (tt % 2) + half][:],
                        xT(dt)[:, tt * 128:(tt + 1) * 128],
                        WVO[:, dt * 1024 + half * 512:dt * 1024 + (half + 1) * 512],
                        start=(dt == 0), stop=(dt == 7), skip_group_check=True))
                return i
            p.add("tensor", mmv, reads=[xTb] + [RW.buf(("wv", dt))
                                               for dt in range(8)],
                  writes=[pv[half]])
            def evv(v, tt=tt, half=half):
                return v.tensor_copy(
                    out=BIGC[:, tt * 1040 + half * 520:
                             tt * 1040 + (half + 1) * 520].rearrange(
                        "p (h c) -> p h c", c=65)[:, :, 0:64],
                    in_=pA[2 * (tt % 2) + half][:].rearrange(
                        "p (h c) -> p h c", c=64))
            p.add("vector", evv, reads=[pv[half]],
                  writes=[RC.buf(("v", tt, half))])

    v_tile = lambda tt: BIGC[:, tt * 1040:(tt + 1) * 1040]

    for pair in range(8):
        for a in range(2):
            pqk = [Rp.buf(("pA", c)) for c in range(4)]
            def mmqk(t, pair=pair, a=a):
                i = []
                for kt in range(8):
                    for c in range(4):
                        i.append(t.matmul(
                            pA[c][:],
                            BIGA[:, kt * 2048 + pair * 256 + a * 128:
                                 kt * 2048 + pair * 256 + (a + 1) * 128],
                            xT(kt)[:, c * 512:(c + 1) * 512],
                            start=(kt == 0), stop=(kt == 7),
                            skip_group_check=True))
                return i
            p.add("tensor", mmqk,
                  reads=[xTb] + [RA.buf(("wqk", kt)) for kt in range(8)],
                  writes=pqk)
            qkb = Rsm.buf(("qk", pair, a))
            for c in range(4):
                p.add("vector", lambda v, c=c, a=a: v.tensor_copy(
                    out=qk_b[:, a * 2048 + c * 512:a * 2048 + (c + 1) * 512],
                    in_=pA[c][:]), reads=[pqk[c]], writes=[qkb])

        for hp in range(2):
            h = pair * 2 + hp
            base = hp * 64
            qkb_q = Rsm.buf(("qk", pair, 0))
            qkb_k = Rsm.buf(("qk", pair, 1))
            pav = [Rp.buf(("pA", c)) for c in range(4)]
            for kt in range(16):
                for c in range(4):
                    psb = Rp.buf(("pB", c % 2))
                    p.add("tensor", lambda t, kt=kt, c=c, base=base: t.matmul(
                        pB[c % 2][:],
                        qk_b[base:base + 64, 2048 + kt * 128:2048 + (kt + 1) * 128],
                        qk_b[base:base + 64, c * 512:(c + 1) * 512],
                        start=True, stop=True, skip_group_check=True),
                        reads=[qkb_q, qkb_k], writes=[psb])
                    atb = Rsm.buf(("at", c))
                    p.add("scalar", lambda sc, c=c: sc.activation(
                        out=at_t[c][:], in_=pB[c % 2][:], func=AF.Exp,
                        scale=0.125), reads=[psb], writes=[atb])
                    p.add("tensor", lambda t, h=h, kt=kt, c=c: t.matmul(
                        pA[c][0:65, :],
                        v_tile(kt)[:, h * 65:(h + 1) * 65],
                        at_t[c][:],
                        start=(kt == 0), stop=(kt == 15),
                        skip_group_check=True),
                        reads=[atb, RC.buf(("v", kt, h // 8)),
                               RC.buf(("vones", kt))],
                        writes=[pav[c]])
            ct = h // 2
            avb = Rsm.buf("av_sb")
            for c in range(4):
                p.add("vector", lambda v, c=c: v.tensor_copy(
                    out=av_sb[:, c * 512:(c + 1) * 512], in_=pA[c][0:65, :]),
                    reads=[pav[c]], writes=[avb])
            p.add("vector", lambda v: v.reciprocal(
                out=av_sb[64:65, :], in_=av_sb[64:65, :]),
                reads=[avb], writes=[avb], fence=True)
            for c in range(4):
                pdv = Rp.buf(("pB", c % 2))
                p.add("tensor", lambda t, c=c: t.matmul(
                    pB[c % 2][0:64, :], selbc[:],
                    av_sb[:, c * 512:(c + 1) * 512], start=True, stop=True,
                    skip_group_check=True),
                    reads=[avb, Rsm.buf("selbc")], writes=[pdv])
                if hp == 0:
                    p.add("vector", lambda v, c=c, ct=ct: v.tensor_tensor(
                        out=BIGD[0:64, ct * 2048 + c * 512:
                                 ct * 2048 + (c + 1) * 512],
                        in0=av_sb[0:64, c * 512:(c + 1) * 512],
                        in1=pB[c % 2][0:64, :], op=ALU.mult),
                        reads=[avb, pdv], writes=[RD.buf(("cT", ct))])
                else:
                    stb = Rsm.buf(("stage", c % 2))
                    p.add("vector", lambda v, c=c: v.tensor_tensor(
                        out=stage_sb[c % 2][:],
                        in0=av_sb[0:64, c * 512:(c + 1) * 512],
                        in1=pB[c % 2][0:64, :], op=ALU.mult),
                        reads=[avb, pdv], writes=[stb])
                    p.add("gpsimd", lambda g, c=c, ct=ct: g.dma_start(
                        out=BIGD[64:128, ct * 2048 + c * 512:
                                 ct * 2048 + (c + 1) * 512],
                        in_=stage_sb[c % 2][:]),
                        reads=[stb], writes=[RD.buf(("cT", ct))], dma=True)

    wvo_dead = RW.retire()
    for ct in range(8):
        p.add("gpsimd", lambda g, ct=ct: g.dma_start(
            out=WVO[:, ct * 1024:(ct + 1) * 1024],
            in_=ap["wo"][ct * 128:(ct + 1) * 128, :]),
            writes=[RW.buf(("wo", ct))], deps=wvo_dead, dma=True)

    store_steps = []
    for tt in range(16):
        po = [Rp.buf(("pA", 2 * (tt % 2) + k)) for k in range(2)]
        for half in range(2):
            def mmo(t, tt=tt, half=half):
                i = []
                for ct in range(8):
                    i.append(t.matmul(
                        pA[2 * (tt % 2) + half][:],
                        BIGD[:, ct * 2048 + tt * 128:ct * 2048 + (tt + 1) * 128],
                        WVO[:, ct * 1024 + half * 512:ct * 1024 + (half + 1) * 512],
                        start=(ct == 0), stop=(ct == 7), skip_group_check=True))
                return i
            p.add("tensor", mmo,
                  reads=[RD.buf(("cT", c2)) for c2 in range(8)] +
                        [RW.buf(("wo", c2)) for c2 in range(8)],
                  writes=[po[half]])
        xrb = Rf.buf(("xr", tt % 2))
        p.add("sync", lambda e, tt=tt: e.dma_start(
            out=f32_t[tt % 2][:], in_=ap["x"][tt * 128:(tt + 1) * 128, :]),
            writes=[xrb], dma=True)
        sb = Rf.buf(("s", tt % 2))
        def addres(v, tt=tt):
            i = []
            for half in range(2):
                i.append(v.tensor_tensor(
                    out=f32_t[2 + tt % 2][:, half * 512:(half + 1) * 512],
                    in0=f32_t[tt % 2][:, half * 512:(half + 1) * 512],
                    in1=pA[2 * (tt % 2) + half][:], op=ALU.add))
            return i
        p.add("vector", addres, reads=[xrb] + po, writes=[sb])
        if dbg and tt == 0:
            p.add("sync", lambda e: e.dma_start(out=dbg["d_s"],
                  in_=f32_t[2][:]), reads=[sb], dma=True)
        outb = xrb
        _ln_emit(p, eps_sb[:], f32_t[2 + tt % 2][:], sb,
                 f32_t[tt % 2][:], outb,
                 f32_t[4][:], Rf.buf("sq"),
                 stat[:, (tt % 2) * 8:(tt % 2) * 8 + 8], Rf.buf(("st", tt % 2)),
                 extra_reads=[Rsm.buf("eps")])
        if dbg and tt == 0:
            p.add("sync", lambda e: e.dma_start(out=dbg["d_stat"],
                  in_=stat[:]), reads=[Rf.buf(("st", 0))], dma=True)
        stc = p.add("sync", lambda e, tt=tt: e.dma_start(
            out=x2_out[tt * 128:(tt + 1) * 128, :],
            in_=f32_t[tt % 2][:]), reads=[outb], writes=[Rx2.buf(tt)],
            dma=True)
        store_steps.append(stc)

    if dbg:
        dsteps = []
        dsteps.append(p.add("sync", lambda e: e.dma_start(out=dbg["d_qk"], in_=qk_b[:]),
              reads=[Rsm.buf(("qk", 7, 0)), Rsm.buf(("qk", 7, 1))], dma=True))
        dsteps.append(p.add("sync", lambda e: e.dma_start(out=dbg["d_v"], in_=BIGC[:]),
              reads=[RC.buf(("v", tt2, hf)) for tt2 in range(16) for hf in range(2)], dma=True))
        dsteps.append(p.add("sync", lambda e: e.dma_start(out=dbg["d_ct"], in_=BIGD[:]),
              reads=[RD.buf(("cT", c2)) for c2 in range(8)], dma=True))
        dsteps.append(p.add("sync", lambda e: e.dma_start(out=dbg["d_av"], in_=av_sb[:]),
              reads=[Rsm.buf("av_sb")], dma=True))
        dsteps.append(p.add("sync", lambda e: e.dma_start(out=dbg["d_xT"], in_=BIGB[:]),
              reads=[RB.buf("xT")], dma=True))
        store_steps += dsteps
    ra_dead = RA.retire()
    rb_dead = RB.retire()
    rc_dead = RC.retire()
    rd_dead = RD.retire()

    def fresh(region, key, dead):
        b = region.buf(key)
        b.last_reads = list(dead)
        return b
    for k in [("PT", 0), ("PT", 1), ("QT", 0), ("QT", 1)]:
        fresh(RC, k, rc_dead)
    fresh(RB, "xT2", rb_dead)
    p.add("gpsimd", lambda g: g.dma_start(
        out=BIGC[:, 0:2048],
        in_=ap["u1"].rearrange("(kt p) r -> p kt r", p=128)),
        writes=[RC.buf("u1")], deps=rc_dead, dma=True)
    p.add("gpsimd", lambda g: g.dma_start(
        out=BIGC[:, 2048:4096],
        in_=ap["v2"].rearrange("(kt p) r -> p kt r", p=128)),
        writes=[RC.buf("v2")], deps=rc_dead, dma=True)
    p.add("gpsimd", lambda g: g.dma_start(
        out=BIGD[:, 0:8192],
        in_=ap["v1"].rearrange("(kt p) f -> p kt f", p=128)),
        writes=[RD.buf("v1")], deps=rd_dead, dma=True)
    p.add("gpsimd", lambda g: g.dma_start(
        out=BIGD[:, 8192:16384],
        in_=ap["u2"].rearrange("(kt p) r -> p kt r", p=128)),
        writes=[RD.buf("u2")], deps=rd_dead, dma=True)

    for tt in range(16):
        xbb = RB.buf(("xb2", tt % 2))
        p.add("gpsimd", lambda g, tt=tt: g.dma_start(
            out=xb_t[tt % 2][:], in_=x2_out[tt * 128:(tt + 1) * 128, :]),
            reads=[Rx2.buf(tt)], writes=[xbb],
            deps=(rb_dead if tt < 2 else ()), dma=True)
        transpose_to(p, xb_t[tt % 2], xbb,
                     (lambda tt: lambda dt: BIGB[:, dt * 2048 + tt * 128:
                                                 dt * 2048 + (tt + 1) * 128])(tt),
                     RB.buf("xT2"), 8)
    xT2 = lambda dt: BIGB[:, dt * 2048:(dt + 1) * 2048]
    xT2b = RB.buf("xT2")

    for mh in range(2):
        ppt = [Rp.buf(("pA", c)) for c in range(4)]
        def mmpt(t, mh=mh):
            i = []
            for kt in range(8):
                for c in range(4):
                    i.append(t.matmul(
                        pA[c][:],
                        BIGC[:, kt * 256 + mh * 128:kt * 256 + (mh + 1) * 128],
                        xT2(kt)[:, c * 512:(c + 1) * 512],
                        start=(kt == 0), stop=(kt == 7),
                        skip_group_check=True))
            return i
        p.add("tensor", mmpt, reads=[xT2b, RC.buf("u1")], writes=ppt)
        for c in range(4):
            p.add("vector", lambda v, mh=mh, c=c: v.tensor_copy(
                out=BIGC[:, 4096 + mh * 2048 + c * 512:
                         4096 + mh * 2048 + (c + 1) * 512],
                in_=pA[c][:]), reads=[ppt[c]], writes=[RC.buf(("PT", mh))])

    for tc in range(4):
        pqt = [Rp.buf(("pA", m2)) for m2 in range(2)]
        for mt in range(32):
            phb = Rp.buf(("pB", mt % 2))
            def mmh(t, tc=tc, mt=mt):
                i = []
                for rt in range(2):
                    i.append(t.matmul(
                        pB[mt % 2][:],
                        BIGD[:, rt * 4096 + mt * 128:rt * 4096 + (mt + 1) * 128],
                        BIGC[:, 4096 + rt * 2048 + tc * 512:
                             4096 + rt * 2048 + (tc + 1) * 512],
                        start=(rt == 0), stop=(rt == 1),
                        skip_group_check=True))
                return i
            p.add("tensor", mmh, reads=[RD.buf("v1"), RC.buf(("PT", 0)),
                                        RC.buf(("PT", 1))], writes=[phb])
            hb = Rsm.buf(("h", mt % 2))
            p.add("scalar", lambda sc, mt=mt: sc.activation(
                out=h_t[mt % 2][:], in_=pB[mt % 2][:],
                func=AF.Gelu_apprx_tanh, bias=b1_sb[:, mt:mt + 1]),
                reads=[phb, Rsm.buf("b1")], writes=[hb])
            def mmqt(t, mt=mt):
                i = []
                for m2 in range(2):
                    i.append(t.matmul(
                        pA[m2][:],
                        BIGD[:, 8192 + mt * 256 + m2 * 128:
                             8192 + mt * 256 + (m2 + 1) * 128],
                        h_t[mt % 2][:],
                        start=(mt == 0), stop=(mt == 31),
                        skip_group_check=True))
                return i
            p.add("tensor", mmqt, reads=[hb, RD.buf("u2")], writes=pqt)
        for m2 in range(2):
            p.add("vector", lambda v, tc=tc, m2=m2: v.tensor_copy(
                out=BIGC[:, 8192 + m2 * 2048 + tc * 512:
                         8192 + m2 * 2048 + (tc + 1) * 512],
                in_=pA[m2][:]), reads=[pqt[m2]], writes=[RC.buf(("QT", m2))])

    for tt in range(16):
        po = [Rp.buf(("pA", 2 + half)) for half in range(2)]
        for half in range(2):
            def mmf(t, tt=tt, half=half):
                i = []
                for rt2 in range(2):
                    i.append(t.matmul(
                        pA[2 + half][:],
                        BIGC[:, 8192 + rt2 * 2048 + tt * 128:
                             8192 + rt2 * 2048 + (tt + 1) * 128],
                        BIGC[:, 2048 + rt2 * 1024 + half * 512:
                             2048 + rt2 * 1024 + (half + 1) * 512],
                        start=(rt2 == 0), stop=(rt2 == 1),
                        skip_group_check=True))
                return i
            p.add("tensor", mmf, reads=[RC.buf(("QT", 0)), RC.buf(("QT", 1)),
                                        RC.buf("v2")], writes=[po[half]])
        xrb = Rf.buf(("xr", tt % 2))
        p.add("sync", lambda e, tt=tt: e.dma_start(
            out=f32_t[tt % 2][:], in_=x2_out[tt * 128:(tt + 1) * 128, :]),
            reads=[Rx2.buf(tt)], writes=[xrb], dma=True)
        prtb = Rf.buf(("prt", tt % 2))
        p.add("gpsimd", lambda g, tt=tt: g.tensor_tensor(
            out=f32_t[2 + tt % 2][:], in0=f32_t[tt % 2][:], in1=b2bc[:],
            op=ALU.add), reads=[xrb, Rsm.buf("b2bc")], writes=[prtb])
        sb = xrb
        def adds(v, tt=tt):
            i = []
            for half in range(2):
                i.append(v.tensor_tensor(
                    out=f32_t[tt % 2][:, half * 512:(half + 1) * 512],
                    in0=f32_t[2 + tt % 2][:, half * 512:(half + 1) * 512],
                    in1=pA[2 + half][:], op=ALU.add))
            return i
        p.add("vector", adds, reads=[prtb] + po, writes=[sb])
        _ln_emit(p, eps_sb[:], f32_t[tt % 2][:], sb,
                 f32_t[2 + tt % 2][:], prtb,
                 f32_t[4][:], Rf.buf("sq"),
                 stat[:, (tt % 2) * 8:(tt % 2) * 8 + 8], Rf.buf(("st", tt % 2)),
                 extra_reads=[Rsm.buf("eps")])
        stc = p.add("sync", lambda e, tt=tt: e.dma_start(
            out=x2_out[tt * 128:(tt + 1) * 128, :],
            in_=f32_t[2 + tt % 2][:]), reads=[prtb], writes=[Rx2.buf(tt)],
            dma=True)
        store_steps.append(stc)

    p.add("gpsimd", lambda e: None, deps=store_steps)
    return ap, x2_out, p, st


def _prep_l0(inputs, b):
    wqkv = np.asarray(inputs["L0_attn_w_qkv"])
    m = {
        "x": np.ascontiguousarray(np.asarray(inputs["L0_attn_inputs"])[b]),
        "wqk": np.ascontiguousarray(
            wqkv[:, :2].transpose(2, 0, 1, 3)
            .reshape(1024, 8, 2, 2, 64)
            .transpose(0, 1, 3, 2, 4)
            .reshape(8, 128, 2048)),
        "wv": np.ascontiguousarray(
            wqkv[:, 2].transpose(1, 0, 2).reshape(8, 128, 1024)),
        "wo": np.ascontiguousarray(np.asarray(inputs["L0_attn_w_o"])),
        "u1": np.ascontiguousarray(np.asarray(inputs["L0_ffn_U1"])),
        "v1": np.ascontiguousarray(np.asarray(inputs["L0_ffn_V1"])),
        "b1t": np.ascontiguousarray(
            np.asarray(inputs["L0_ffn_b1"]).reshape(32, 128).T),
        "u2": np.ascontiguousarray(np.asarray(inputs["L0_ffn_U2"])),
        "v2": np.ascontiguousarray(np.asarray(inputs["L0_ffn_V2"])),
        "b2r": np.ascontiguousarray(
            np.asarray(inputs["L0_ffn_b2"]).reshape(1, 1024)),
        "identf": np.eye(128, dtype=np.float32),
        "selbcf": np.concatenate([np.zeros((64, 64), np.float32),
                                  np.ones((1, 64), np.float32)], axis=0),
        "epsc": np.full((128, 1), 1e-5, dtype=np.float32),
        "onesf": np.ones((1, 128), dtype=np.float32),
    }
    return {k: np.ascontiguousarray(v.astype(np.float32, copy=False))
            for k, v in m.items()}


def build_l2a(nc):
    ap = {}
    for name, shape in [("xcT", [2, D, 1024]), ("eta", [1, 1]),
                        ("identf", [128, 128]), ("onesf", [1, 128]),
                        ("self", [8, 2048])]:
        ap[name] = nc.dram_tensor(name, shape, F32, kind="ExternalInput").ap()
    b_out = nc.dram_tensor("Bout", [2, 1024, 1024], BF16,
                           kind="ExternalOutput").ap()
    m_out = nc.dram_tensor("Mout", [2, 8, 128, 128], BF16,
                           kind="ExternalOutput").ap()
    import os
    dbg = {}
    if os.environ.get("K_DEBUG"):
        for n, shp, dt in [("d_msb", [128, 128], BF16),
                           ("d_idb", [128, 128], BF16),
                           ("d_stg", [8, 128], BF16)]:
            dbg[n] = nc.dram_tensor(n, shp, dt, kind="ExternalOutput").ap()

    st = ExitStack()
    E = st.enter_context
    XcT = [E(nc.sbuf_tensor(f"XcT{i}", [128, 8192], BF16)) for i in range(2)]
    B_sb = [E(nc.sbuf_tensor(f"Bsb{i}", [128, 8192], BF16)) for i in range(2)]
    M_sb = [E(nc.sbuf_tensor(f"Msb{i}", [128, 128], BF16)) for i in range(2)]
    Mo_sb = [E(nc.sbuf_tensor(f"Mo{i}", [128, 128], BF16)) for i in range(2)]
    stage8 = [E(nc.sbuf_tensor(f"stg{i}", [8, 128], BF16)) for i in range(2)]
    sel_b = E(nc.sbuf_tensor("sel_b", [8, 2048], BF16))
    identb = E(nc.sbuf_tensor("identb", [128, 128], BF16))
    misc = E(nc.sbuf_tensor("misc", [128, 2304], F32))
    mcvec = E(nc.sbuf_tensor("mcvec", [128, 1], F32))
    ones_sb = E(nc.sbuf_tensor("ones_sb", [1, 128], F32))
    eta_sb = E(nc.sbuf_tensor("eta_sb", [1, 1], F32))

    pA = [E(nc.psum_tensor(f"pA{i}", [128, 512], F32)) for i in range(2)]
    pM = [E(nc.psum_tensor(f"pM{i}", [128, 128], F32)) for i in range(2)]
    pS = [E(nc.psum_tensor(f"pS{i}", [8, 128], F32)) for i in range(2)]
    pE1 = E(nc.psum_tensor("pE1", [128, 1], F32))
    pC = E(nc.psum_tensor("pC", [128, 128], BF16))

    p = Prog()
    R = Region()
    Rp = Region()

    p.add("sync", lambda e: e.dma_start(out=ones_sb[:], in_=ap["onesf"]),
          writes=[R.buf("ones")], dma=True)
    p.add("sync", lambda e: e.dma_start(out=eta_sb[:], in_=ap["eta"]),
          writes=[R.buf("eta")], dma=True)
    p.add("sync", lambda e: e.dma_start(out=misc[:, 0:128], in_=ap["identf"]),
          writes=[R.buf("mi")], dma=True)
    p.add("scalar", lambda s: s.copy(out=identb[:], in_=misc[:, 0:128]),
          reads=[R.buf("mi")], writes=[R.buf("identb")])
    p.add("sync", lambda e: e.dma_start(out=misc[0:8, 256:2304], in_=ap["self"]),
          writes=[R.buf("ms")], dma=True)
    p.add("scalar", lambda s: s.copy(out=sel_b[:], in_=misc[0:8, 256:2304]),
          reads=[R.buf("ms")], writes=[R.buf("sel")])
    p.add("tensor", lambda t: t.matmul(pE1[:], ones_sb[:], eta_sb[:],
                                       start=True, stop=True,
                                       skip_group_check=True),
          reads=[R.buf("ones"), R.buf("eta")], writes=[Rp.buf("pE1")])
    p.add("scalar", lambda sc: sc.mul(
        out=mcvec[:], in_=pE1[:], mul=-1.0 / 4096.0),
        reads=[Rp.buf("pE1")], writes=[R.buf("mc")])

    for j in range(2):
        xb = R.buf(("xcT", j))
        p.add("gpsimd", lambda g, j=j: g.dma_start(
            out=XcT[j][:],
            in_=ap["xcT"][j].rearrange("(kt p) r -> p kt r", p=128)),
            writes=[xb], dma=True)
        for mt in range(8):
            for c2 in range(2):
                pb = Rp.buf(("pA", (2 * mt + c2) % 2))
                def mma(t, j=j, mt=mt, c2=c2):
                    i = []
                    for kt in range(8):
                        i.append(t.matmul(
                            pA[(2 * mt + c2) % 2][:],
                            XcT[j][:, kt * 1024 + mt * 128:
                                 kt * 1024 + (mt + 1) * 128],
                            XcT[j][:, kt * 1024 + c2 * 512:
                                 kt * 1024 + (c2 + 1) * 512],
                            start=(kt == 0), stop=(kt == 7),
                            skip_group_check=True))
                    return i
                p.add("tensor", mma, reads=[xb], writes=[pb])
                p.add("vector", lambda v, j=j, mt=mt, c2=c2: v.tensor_scalar(
                    out=B_sb[j][:, mt * 1024 + c2 * 512:
                               mt * 1024 + (c2 + 1) * 512],
                    in0=pA[(2 * mt + c2) % 2][:], scalar1=mcvec[:],
                    scalar2=None, op0=ALU.mult),
                    reads=[pb, R.buf("mc")], writes=[R.buf(("B", j))])
        p.add("sync", lambda e, j=j: e.dma_start(
            out=b_out[j].rearrange("(mt p) n -> p mt n", p=128),
            in_=B_sb[j][:]), reads=[R.buf(("B", j))], dma=True)

        for i in range(8):
            m_i = (j * 8 + i) % 2
            mb = R.buf(("M", m_i))
            pmb = Rp.buf(("pM", m_i))
            p.add("vector", lambda v, m_i=m_i: v.tensor_copy(
                out=M_sb[m_i][:], in_=identb[:]),
                reads=[R.buf("identb")], writes=[mb])
            dref = B_sb[j][:, i * 1024 + i * 128:i * 1024 + (i + 1) * 128]
            for s_ in range(1, 16):
                psb = Rp.buf(("pS", s_ % 2))
                p.add("tensor", lambda t, s_=s_, dref=dref, m_i=m_i: t.matmul(
                    pS[s_ % 2][0:8, 0:8 * s_],
                    dref[0:8 * s_, s_ * 8:(s_ + 1) * 8],
                    M_sb[m_i][0:8 * s_, 0:8 * s_],
                    start=True, stop=True, skip_group_check=True),
                    reads=[R.buf(("B", j)), mb], writes=[psb])
                sgb = R.buf(("stg", s_ % 2))
                p.add("vector", lambda v, s_=s_: v.tensor_copy(
                    out=stage8[s_ % 2][0:8, 0:8 * s_],
                    in_=pS[s_ % 2][0:8, 0:8 * s_]),
                    reads=[psb], writes=[sgb])
                p.add("tensor", lambda t, s_=s_, m_i=m_i: t.matmul(
                    pM[m_i][0:128, 0:8 * s_],
                    sel_b[:, s_ * 128:(s_ + 1) * 128],
                    stage8[s_ % 2][0:8, 0:8 * s_],
                    start=(s_ == 1), stop=(s_ == 15),
                    skip_group_check=True),
                    reads=[sgb, R.buf("sel")], writes=[pmb])
                p.add("vector", lambda v, s_=s_, m_i=m_i: v.tensor_tensor(
                    out=M_sb[m_i][:, 0:8 * s_], in0=pM[m_i][0:128, 0:8 * s_],
                    in1=identb[:, 0:8 * s_], op=ALU.add),
                    reads=[pmb, R.buf("identb")], writes=[mb], fence=True)
            if dbg and j == 0 and i == 0:
                p.add("sync", lambda e, m_i=m_i: e.dma_start(
                    out=dbg["d_msb"], in_=M_sb[m_i][:]), reads=[mb], dma=True)
                p.add("sync", lambda e: e.dma_start(
                    out=dbg["d_idb"], in_=identb[:]),
                    reads=[R.buf("identb")], dma=True)
                p.add("sync", lambda e: e.dma_start(
                    out=dbg["d_stg"], in_=stage8[1][:]),
                    reads=[R.buf(("stg", 1))], dma=True)
            pcb = Rp.buf("pC")
            p.add("tensor", lambda t, m_i=m_i: t.transpose(
                pC[:], M_sb[m_i][:], identb[:]),
                reads=[mb, R.buf("identb")], writes=[pcb])
            mo = R.buf(("Mo", m_i))
            p.add("vector", lambda v, m_i=m_i: v.tensor_copy(
                out=Mo_sb[m_i][:], in_=pC[:]), reads=[pcb], writes=[mo])
            p.add("sync", lambda e, j=j, i=i, m_i=m_i: e.dma_start(
                out=m_out[j, i], in_=Mo_sb[m_i][:]), reads=[mo], dma=True)

    outs = [s2 for s2 in p.steps["sync"] if s2.dma]
    p.add("gpsimd", lambda e: None, deps=outs)
    return ap, p, st


def _prep_l2a(x2_full, inputs, k):
    sel = np.zeros((8, 16, 128), np.float32)
    for s_ in range(1, 16):
        for kk in range(8):
            sel[kk, s_, 8 * s_ + kk] = 1.0
    xcT = np.stack([
        np.ascontiguousarray(
            x2_full[:, c * 128:(c + 1) * 128, :].transpose(2, 1, 0)
            .reshape(1024, 1024))
        for c in (2 * k, 2 * k + 1)])
    return {
        "xcT": xcT.astype(np.float32),
        "eta": np.asarray(inputs["L1_attn_eta"], np.float32).reshape(1, 1),
        "identf": np.eye(128, dtype=np.float32),
        "onesf": np.ones((1, 128), np.float32),
        "self": np.ascontiguousarray(sel.reshape(8, 2048)),
    }


def build_l2b(nc):
    ap = {}
    for name, shape, dt in [
        ("xcT", [NCHUNK, D, 1024], F32), ("xcr", [NCHUNK, 1024, D], F32),
        ("Ball", [NCHUNK, 1024, 1024], BF16), ("Mall", [NCHUNK, 8, 128, 128], BF16),
        ("w0t", [D, ESL], F32), ("wvt", [D, ESL], F32), ("bvr", [1, ESL], F32),
        ("eta", [1, 1], F32), ("onesf", [1, 128], F32),
    ]:
        ap[name] = nc.dram_tensor(name, shape, dt, kind="ExternalInput").ap()
    y_out = nc.dram_tensor("yout", [NCHUNK, 1024, ESL], F32,
                           kind="ExternalOutput").ap()

    st = ExitStack()
    E = st.enter_context
    XcT = [E(nc.sbuf_tensor(f"XcT{i}", [128, 8192], BF16)) for i in range(2)]
    Xcr = [E(nc.sbuf_tensor(f"Xcr{i}", [128, 8192], BF16)) for i in range(2)]
    B_sb = [E(nc.sbuf_tensor(f"Bsb{i}", [128, 8192], BF16)) for i in range(2)]
    M_sb = [E(nc.sbuf_tensor(f"Msb{i}", [128, 1024], BF16)) for i in range(2)]
    W_f32 = E(nc.sbuf_tensor("W_f32", [128, 1024], F32))
    Wt_b = E(nc.sbuf_tensor("Wt_b", [128, 1024], BF16))
    Wv_b = E(nc.sbuf_tensor("Wv_b", [128, 1024], BF16))
    V_f32 = E(nc.sbuf_tensor("V_f32", [128, 1024], F32))
    R_f32 = E(nc.sbuf_tensor("R_f32", [128, 1024], F32))
    E_b = E(nc.sbuf_tensor("E_b", [128, 1024], BF16))
    Y_f32 = [E(nc.sbuf_tensor(f"Y{i}", [128, 1024], F32)) for i in range(2)]
    G_b = [E(nc.sbuf_tensor(f"G{i}", [128, 128], BF16)) for i in range(2)]
    bvbc = E(nc.sbuf_tensor("bvbc", [128, 128], F32))
    mcvec = E(nc.sbuf_tensor("mcvec", [128, 1], F32))
    ones_sb = E(nc.sbuf_tensor("ones_sb", [1, 128], F32))
    eta_sb = E(nc.sbuf_tensor("eta_sb", [1, 1], F32))
    bv_sb = E(nc.sbuf_tensor("bv_sb", [1, 128], F32))

    pZ = [E(nc.psum_tensor(f"pZ{i}", [128, 128], F32)) for i in range(2)]
    pX = E(nc.psum_tensor("pX", [128, 128], F32))
    pE = E(nc.psum_tensor("pE", [128, 128], F32))
    pW = [E(nc.psum_tensor(f"pW{i}", [128, 128], F32)) for i in range(2)]
    pB1 = E(nc.psum_tensor("pB1", [128, 128], F32))

    p = Prog()
    R = Region()
    Rp = Region()

    p.add("sync", lambda e: e.dma_start(out=ones_sb[:], in_=ap["onesf"]),
          writes=[R.buf("ones")], dma=True)
    p.add("sync", lambda e: e.dma_start(out=eta_sb[:], in_=ap["eta"]),
          writes=[R.buf("eta")], dma=True)
    p.add("sync", lambda e: e.dma_start(out=bv_sb[:], in_=ap["bvr"]),
          writes=[R.buf("bv")], dma=True)
    p.add("sync", lambda e: e.dma_start(
        out=W_f32[:], in_=ap["w0t"].rearrange("(kt p) e -> p kt e", p=128)),
        writes=[R.buf("W")], dma=True)
    p.add("scalar", lambda s: s.copy(out=Wt_b[:], in_=W_f32[:]),
          reads=[R.buf("W")], writes=[R.buf("Wb")])
    p.add("gpsimd", lambda g: g.dma_start(
        out=Wv_b[:], in_=ap["wvt"].rearrange("(kt p) e -> p kt e", p=128)),
        writes=[R.buf("Wv")], dma=True)
    p.add("tensor", lambda t: t.matmul(pB1[:, 0:1], ones_sb[:], eta_sb[:],
                                       start=True, stop=True,
                                       skip_group_check=True),
          reads=[R.buf("ones"), R.buf("eta")], writes=[Rp.buf("pB1")])
    p.add("scalar", lambda sc: sc.mul(
        out=mcvec[:], in_=pB1[:, 0:1], mul=-1.0 / 4096.0),
        reads=[Rp.buf("pB1")], writes=[R.buf("mc")])
    p.add("tensor", lambda t: t.matmul(pB1[:], ones_sb[:], bv_sb[:],
                                       start=True, stop=True,
                                       skip_group_check=True),
          reads=[R.buf("bv"), R.buf("ones"), Rp.buf("pB1"), R.buf("mc")],
          writes=[Rp.buf("pB1")])
    p.add("scalar", lambda sc: sc.copy(out=bvbc[:], in_=pB1[:]),
          reads=[Rp.buf("pB1")], writes=[R.buf("bvbc")])

    store_steps = []
    for c in range(NCHUNK):
        par = c % 2
        xtb, xrb2, bb, mb = (R.buf(("xcT", par)), R.buf(("xcr", par)),
                             R.buf(("B", par)), R.buf(("M", par)))
        p.add("gpsimd", lambda g, c=c, par=par: g.dma_start(
            out=XcT[par][:],
            in_=ap["xcT"][c].rearrange("(kt p) r -> p kt r", p=128)),
            writes=[xtb], dma=True)
        p.add("gpsimd", lambda g, c=c, par=par: g.dma_start(
            out=Xcr[par][:],
            in_=ap["xcr"][c].rearrange("(rt p) d -> p rt d", p=128)),
            writes=[xrb2], dma=True)
        p.add("sync", lambda e, c=c, par=par: e.dma_start(
            out=B_sb[par][:],
            in_=ap["Ball"][c].rearrange("(mt p) n -> p mt n", p=128)),
            writes=[bb], dma=True)
        p.add("sync", lambda e, c=c, par=par: e.dma_start(
            out=M_sb[par][:],
            in_=ap["Mall"][c].rearrange("i p f -> p i f")),
            writes=[mb], dma=True)

        vb = R.buf("V")
        rb = R.buf("R")
        for rt in range(8):
            pzb = Rp.buf(("pZ", rt % 2))
            def mmv(t, par=par, rt=rt):
                i = []
                for kt in range(8):
                    i.append(t.matmul(
                        pZ[rt % 2][:],
                        XcT[par][:, kt * 1024 + rt * 128:
                                 kt * 1024 + (rt + 1) * 128],
                        Wv_b[:, kt * 128:(kt + 1) * 128],
                        start=(kt == 0), stop=(kt == 7),
                        skip_group_check=True))
                return i
            p.add("tensor", mmv, reads=[xtb, R.buf("Wv")], writes=[pzb])
            p.add("vector", lambda v, rt=rt, par=par: v.tensor_tensor(
                out=V_f32[:, rt * 128:(rt + 1) * 128], in0=pZ[rt % 2][:],
                in1=bvbc[:], op=ALU.add),
                reads=[pzb, R.buf("bvbc")], writes=[vb])
        for rt in range(8):
            pzb = Rp.buf(("pZ", rt % 2))
            def mmz(t, par=par, rt=rt):
                i = []
                for kt in range(8):
                    i.append(t.matmul(
                        pZ[rt % 2][:],
                        XcT[par][:, kt * 1024 + rt * 128:
                                 kt * 1024 + (rt + 1) * 128],
                        Wt_b[:, kt * 128:(kt + 1) * 128],
                        start=(kt == 0), stop=(kt == 7),
                        skip_group_check=True))
                return i
            p.add("tensor", mmz, reads=[xtb, R.buf("Wb")], writes=[pzb])
            p.add("vector", lambda v, rt=rt, par=par: v.tensor_tensor(
                out=R_f32[:, rt * 128:(rt + 1) * 128], in0=pZ[rt % 2][:],
                in1=V_f32[:, rt * 128:(rt + 1) * 128], op=ALU.subtract),
                reads=[pzb, vb], writes=[rb])

        eb = R.buf("E")
        yb = R.buf(("Y", par))
        for i in range(8):
            gb = R.buf(("G", i % 2))
            if i == 0:
                p.add("scalar", lambda s, i=i: s.copy(
                    out=G_b[i % 2][:], in_=R_f32[:, 0:128]),
                    reads=[rb], writes=[gb])
            else:
                pxb = Rp.buf("pX")
                def mmx(t, par=par, i=i):
                    ii = []
                    for jj in range(i):
                        ii.append(t.matmul(
                            pX[:],
                            B_sb[par][:, jj * 1024 + i * 128:
                                     jj * 1024 + (i + 1) * 128],
                            E_b[:, jj * 128:(jj + 1) * 128],
                            start=(jj == 0), stop=(jj == i - 1),
                            skip_group_check=True))
                    return ii
                p.add("tensor", mmx, reads=[bb, eb], writes=[pxb])
                p.add("vector", lambda v, i=i: v.tensor_tensor(
                    out=G_b[i % 2][:], in0=pX[:],
                    in1=R_f32[:, i * 128:(i + 1) * 128], op=ALU.add),
                    reads=[pxb, rb], writes=[gb])
            peb = Rp.buf("pE")
            p.add("tensor", lambda t, par=par, i=i: t.matmul(
                pE[:], M_sb[par][:, i * 128:(i + 1) * 128], G_b[i % 2][:],
                start=True, stop=True, skip_group_check=True),
                reads=[mb, gb], writes=[peb])
            p.add("vector", lambda v, i=i: v.tensor_copy(
                out=E_b[:, i * 128:(i + 1) * 128], in_=pE[:]),
                reads=[peb], writes=[eb])
            p.add("vector", lambda v, i=i, par=par: v.tensor_tensor(
                out=Y_f32[par][:, i * 128:(i + 1) * 128], in0=pE[:],
                in1=V_f32[:, i * 128:(i + 1) * 128], op=ALU.add),
                reads=[peb, vb], writes=[yb])

        for dt_ in range(8):
            pwb = Rp.buf(("pW", dt_ % 2))
            def mmw(t, par=par, dt_=dt_):
                i = []
                for rt in range(8):
                    i.append(t.matmul(
                        pW[dt_ % 2][:],
                        Xcr[par][:, rt * 1024 + dt_ * 128:
                                 rt * 1024 + (dt_ + 1) * 128],
                        E_b[:, rt * 128:(rt + 1) * 128],
                        start=(rt == 0), stop=(rt == 7),
                        skip_group_check=True))
                return i
            p.add("tensor", mmw, reads=[xrb2, eb], writes=[pwb])
            p.add("vector", lambda v, dt_=dt_: v.scalar_tensor_tensor(
                out=W_f32[:, dt_ * 128:(dt_ + 1) * 128], in0=pW[dt_ % 2][:],
                scalar=mcvec[:], in1=W_f32[:, dt_ * 128:(dt_ + 1) * 128],
                op0=ALU.mult, op1=ALU.add),
                reads=[pwb, R.buf("mc")], writes=[R.buf("W")])
            p.add("scalar", lambda s, dt_=dt_: s.copy(
                out=Wt_b[:, dt_ * 128:(dt_ + 1) * 128],
                in_=W_f32[:, dt_ * 128:(dt_ + 1) * 128]),
                reads=[R.buf("W")], writes=[R.buf("Wb")])

        stc = p.add("sync", lambda e, c=c, par=par: e.dma_start(
            out=y_out[c].rearrange("(rt p) e -> p rt e", p=128),
            in_=Y_f32[par][:]), reads=[yb], dma=True)
        store_steps.append(stc)

    p.add("gpsimd", lambda e: None, deps=store_steps)
    return ap, p, st


def _prep_l2b(x2_full, Ball, Mall, inputs, k):
    xcT = np.stack([
        np.ascontiguousarray(
            x2_full[:, c * 128:(c + 1) * 128, :].transpose(2, 1, 0)
            .reshape(1024, 1024)) for c in range(NCHUNK)])
    xcr = np.stack([
        np.ascontiguousarray(
            x2_full[:, c * 128:(c + 1) * 128, :].transpose(1, 0, 2)
            .reshape(1024, 1024)) for c in range(NCHUNK)])
    W0 = np.asarray(inputs["L1_attn_W0"])
    Wkv = np.asarray(inputs["L1_attn_Wkv"])
    bkv = np.asarray(inputs["L1_attn_bkv"])
    sl = slice(k * ESL, (k + 1) * ESL)
    return {
        "xcT": xcT.astype(np.float32), "xcr": xcr.astype(np.float32),
        "Ball": Ball, "Mall": Mall,
        "w0t": np.ascontiguousarray(W0[sl, :].T),
        "wvt": np.ascontiguousarray(Wkv[1024:, :][sl, :].T),
        "bvr": np.ascontiguousarray(bkv[1024:][sl].reshape(1, ESL)),
        "eta": np.asarray(inputs["L1_attn_eta"], np.float32).reshape(1, 1),
        "onesf": np.ones((1, 128), np.float32),
    }


def build_l3(nc):
    ap = {}
    for name, shape in [
        ("x2b", [S, D]), ("yb", [S, D]), ("rwt", [D, NE]), ("rbr", [1, NE]),
        ("wgt", [NE, D, NI]), ("wut", [NE, D, NI]), ("wdt", [NE, NI, D]),
        ("identf", [128, 128]), ("onesf", [1, 128]), ("epsc", [128, 1]),
    ]:
        ap[name] = nc.dram_tensor(name, shape, F32, kind="ExternalInput").ap()
    x4_out = nc.dram_tensor("x4", [S, D], F32, kind="ExternalOutput").ap()
    x3d = nc.dram_tensor("x3d", [S, D], F32, kind="Internal").ap()

    st = ExitStack()
    E = st.enter_context
    x3T = E(nc.sbuf_tensor("x3T", [128, 16384], BF16))
    y_acc = E(nc.sbuf_tensor("y_acc", [128, 16384], F32))
    wg_sb = [E(nc.sbuf_tensor(f"wg{i}", [128, 4096], BF16)) for i in range(2)]
    wu_sb = [E(nc.sbuf_tensor(f"wu{i}", [128, 4096], BF16)) for i in range(2)]
    wd_sb = [E(nc.sbuf_tensor(f"wd{i}", [128, 4096], BF16)) for i in range(2)]
    hT_sb = [E(nc.sbuf_tensor(f"hT{i}", [128, 2048], BF16)) for i in range(2)]
    sil_t = [E(nc.sbuf_tensor(f"sil{i}", [128, 512], BF16)) for i in range(2)]
    sco = E(nc.sbuf_tensor("sco", [128, 256], F32))
    comb = E(nc.sbuf_tensor("comb", [128, 256], F32))
    rts = E(nc.sbuf_tensor("rts", [128, 1152], F32))
    rw_sb = E(nc.sbuf_tensor("rw_sb", [128, 128], F32))
    rwb_sb = E(nc.sbuf_tensor("rwb_sb", [128, 128], BF16))
    rbbc = E(nc.sbuf_tensor("rbbc", [128, 16], F32))
    xb_t = [E(nc.sbuf_tensor(f"xb{i}", [128, 1024], BF16)) for i in range(2)]
    f32_t = [E(nc.sbuf_tensor(f"f32_{i}", [128, 1024], F32)) for i in range(6)]
    stat = E(nc.sbuf_tensor("stat", [128, 16], F32))
    identb = E(nc.sbuf_tensor("identb", [128, 128], BF16))
    ones_sb = E(nc.sbuf_tensor("ones_sb", [1, 128], F32))
    eps_sb = E(nc.sbuf_tensor("eps_sb", [128, 1], F32))
    rb_sb = E(nc.sbuf_tensor("rb_sb", [1, 16], F32))

    pG = [E(nc.psum_tensor(f"pG{i}", [128, 512], F32)) for i in range(2)]
    pU = [E(nc.psum_tensor(f"pU{i}", [128, 512], F32)) for i in range(2)]
    pY = [E(nc.psum_tensor(f"pY{i}", [128, 512], F32)) for i in range(2)]
    pR = E(nc.psum_tensor("pR", [128, 512], F32))
    pC = E(nc.psum_tensor("pC", [128, 128], BF16))

    p = Prog()
    R = Region()
    Rp = Region()
    Rd = Region()

    p.add("sync", lambda e: e.dma_start(out=ones_sb[:], in_=ap["onesf"]),
          writes=[R.buf("ones")], dma=True)
    p.add("sync", lambda e: e.dma_start(out=eps_sb[:], in_=ap["epsc"]),
          writes=[R.buf("eps")], dma=True)
    p.add("sync", lambda e: e.dma_start(out=rb_sb[:], in_=ap["rbr"]),
          writes=[R.buf("rb")], dma=True)
    p.add("sync", lambda e: e.dma_start(out=rw_sb[:], in_=ap["identf"]),
          writes=[R.buf("idf")], dma=True)
    p.add("scalar", lambda s: s.copy(out=identb[:], in_=rw_sb[:]),
          reads=[R.buf("idf")], writes=[R.buf("identb")])
    p.add("gpsimd", lambda g: g.dma_start(
        out=rwb_sb[:], in_=ap["rwt"].rearrange("(kt p) e -> p kt e", p=128)),
        writes=[R.buf("rw")], dma=True)
    p.add("tensor", lambda t: t.matmul(pR[:, 0:16], ones_sb[:], rb_sb[:],
                                       start=True, stop=True,
                                       skip_group_check=True),
          reads=[R.buf("ones"), R.buf("rb")], writes=[Rp.buf("pR")])
    p.add("scalar", lambda sc: sc.copy(out=rbbc[:], in_=pR[:, 0:16]),
          reads=[Rp.buf("pR")], writes=[R.buf("rbbc")])

    for tt in range(16):
        xrb = R.buf(("xr", tt % 2))
        yrb = R.buf(("yr", tt % 2))
        p.add("sync", lambda e, tt=tt: e.dma_start(
            out=f32_t[tt % 2][:], in_=ap["x2b"][tt * 128:(tt + 1) * 128, :]),
            writes=[xrb], dma=True)
        p.add("sync", lambda e, tt=tt: e.dma_start(
            out=f32_t[2 + tt % 2][:], in_=ap["yb"][tt * 128:(tt + 1) * 128, :]),
            writes=[yrb], dma=True)
        sb = xrb
        p.add("vector", lambda v, tt=tt: v.tensor_tensor(
            out=f32_t[tt % 2][:], in0=f32_t[tt % 2][:],
            in1=f32_t[2 + tt % 2][:], op=ALU.add),
            reads=[yrb], writes=[sb], fence=True)
        outb = yrb
        _ln_emit(p, eps_sb[:], f32_t[tt % 2][:], sb,
                 f32_t[2 + tt % 2][:], outb,
                 f32_t[4][:], R.buf("sq"),
                 stat[:, (tt % 2) * 8:(tt % 2) * 8 + 8], R.buf(("st", tt % 2)),
                 extra_reads=[R.buf("eps")])
        p.add("sync", lambda e, tt=tt: e.dma_start(
            out=x3d[tt * 128:(tt + 1) * 128, :], in_=f32_t[2 + tt % 2][:]),
            reads=[outb], writes=[Rd.buf(tt)], dma=True)
        xbb = R.buf(("xb", tt % 2))
        p.add("scalar", lambda s, tt=tt: s.copy(
            out=xb_t[tt % 2][:], in_=f32_t[2 + tt % 2][:]),
            reads=[outb], writes=[xbb])
        for dt_ in range(8):
            pcb = Rp.buf("pC")
            p.add("tensor", lambda t, tt=tt, dt_=dt_: t.transpose(
                pC[:], xb_t[tt % 2][:, dt_ * 128:(dt_ + 1) * 128], identb[:]),
                reads=[xbb, R.buf("identb")], writes=[pcb])
            p.add("vector", lambda v, tt=tt, dt_=dt_: v.tensor_copy(
                out=x3T[:, dt_ * 2048 + tt * 128:dt_ * 2048 + (tt + 1) * 128],
                in_=pC[:]), reads=[pcb], writes=[R.buf("x3T")])
        prb = Rp.buf("pR")
        def mmr(t, tt=tt):
            i = []
            for kt in range(8):
                i.append(t.matmul(
                    pR[:, 0:16],
                    x3T[:, kt * 2048 + tt * 128:kt * 2048 + (tt + 1) * 128],
                    rwb_sb[:, kt * 16:(kt + 1) * 16],
                    start=(kt == 0), stop=(kt == 7), skip_group_check=True))
            return i
        p.add("tensor", mmr, reads=[R.buf("x3T"), R.buf("rw")], writes=[prb])
        p.add("scalar", lambda s, tt=tt: s.activation(
            out=sco[:, tt * 16:(tt + 1) * 16], in_=pR[:, 0:16],
            func=AF.Sigmoid), reads=[prb], writes=[R.buf("sco")])

    scob = R.buf("sco")
    rtb = R.buf("rts")
    cmb = R.buf("comb")
    SCH, GSC, PW, THR, GMK, OFF, MK, MX, EQ = (0, 256, 320, 384, 448, 512,
                                               576, 832, 896)

    def vop(fn, reads, writes):
        p.add("vector", fn, reads=reads, writes=writes, fence=True)

    sch = rts[:, SCH:SCH + 256]
    gsc = rts[:, GSC:GSC + 64]
    vop(lambda v: v.tensor_tensor(
        out=sch.rearrange("p (t e) -> p e t", e=16),
        in0=sco[:].rearrange("p (t e) -> p e t", e=16),
        in1=rbbc[:].broadcast_to((128, 16, 16)),
        op=ALU.add), [scob, R.buf("rbbc")], [rtb])
    a_, b_, c_, d_ = (sch[:, 0::4], sch[:, 1::4], sch[:, 2::4], sch[:, 3::4])
    vop(lambda v: v.tensor_tensor(out=rts[:, PW:PW + 64], in0=a_, in1=b_,
                                  op=ALU.max), [rtb], [rtb])
    vop(lambda v: v.tensor_tensor(out=rts[:, THR:THR + 64], in0=c_, in1=d_,
                                  op=ALU.max), [rtb], [rtb])
    vop(lambda v: v.tensor_tensor(out=gsc, in0=rts[:, PW:PW + 64],
                                  in1=rts[:, THR:THR + 64], op=ALU.add),
        [rtb], [rtb])
    vop(lambda v: v.tensor_tensor(out=rts[:, PW:PW + 64], in0=a_, in1=b_,
                                  op=ALU.add), [rtb], [rtb])
    vop(lambda v: v.tensor_tensor(out=rts[:, THR:THR + 64], in0=c_, in1=d_,
                                  op=ALU.add), [rtb], [rtb])
    vop(lambda v: v.tensor_tensor(out=rts[:, PW:PW + 64],
                                  in0=rts[:, PW:PW + 64],
                                  in1=rts[:, THR:THR + 64], op=ALU.max),
        [rtb], [rtb])
    vop(lambda v: v.tensor_tensor(out=gsc, in0=gsc, in1=rts[:, PW:PW + 64],
                                  op=ALU.max), [rtb], [rtb])
    g0, g1 = gsc[:, 0::2], gsc[:, 1::2]
    vop(lambda v: v.tensor_tensor(out=rts[:, PW:PW + 32], in0=g0, in1=g1,
                                  op=ALU.max), [rtb], [rtb])
    vop(lambda v: v.tensor_tensor(out=rts[:, PW + 32:PW + 64], in0=g0, in1=g1,
                                  op=ALU.min), [rtb], [rtb])
    gM0, gM1 = rts[:, PW:PW + 32][:, 0::2], rts[:, PW:PW + 32][:, 1::2]
    gm0, gm1 = (rts[:, PW + 32:PW + 64][:, 0::2],
                rts[:, PW + 32:PW + 64][:, 1::2])
    vop(lambda v: v.tensor_tensor(out=rts[:, THR:THR + 16], in0=gM0, in1=gM1,
                                  op=ALU.min), [rtb], [rtb])
    vop(lambda v: v.tensor_tensor(out=rts[:, THR + 16:THR + 32], in0=gm0,
                                  in1=gm1, op=ALU.max), [rtb], [rtb])
    vop(lambda v: v.tensor_tensor(out=rts[:, THR:THR + 16],
                                  in0=rts[:, THR:THR + 16],
                                  in1=rts[:, THR + 16:THR + 32], op=ALU.max),
        [rtb], [rtb])
    vop(lambda v: v.tensor_tensor(
        out=rts[:, GMK:GMK + 64],
        in0=gsc.rearrange("p (t g) -> p t g", g=4),
        in1=rts[:, THR:THR + 16].broadcast_to((128, 16, 4)), op=ALU.is_ge), [rtb], [rtb])
    vop(lambda v: v.tensor_scalar(
        out=rts[:, OFF:OFF + 64], in0=rts[:, GMK:GMK + 64], scalar1=1e9,
        scalar2=-1e9, op0=ALU.mult, op1=ALU.add), [rtb], [rtb])
    vop(lambda v: v.tensor_tensor(
        out=rts[:, MK:MK + 256], in0=sch.rearrange("p (q e) -> p q e", e=4),
        in1=rts[:, GMK:GMK + 64].broadcast_to((128, 64, 4)), op=ALU.mult), [rtb], [rtb])
    vop(lambda v: v.tensor_tensor(
        out=rts[:, MK:MK + 256],
        in0=rts[:, MK:MK + 256].rearrange("p (q e) -> p q e", e=4),
        in1=rts[:, OFF:OFF + 64].broadcast_to((128, 64, 4)), op=ALU.add), [rtb], [rtb])
    vop(lambda v: v.tensor_copy(out=comb[:], in_=rts[:, MK:MK + 256]),
        [rtb], [cmb])
    for r_ in range(4):
        vop(lambda v, r_=r_: v.tensor_reduce(
            out=rts[:, MX + r_ * 16:MX + (r_ + 1) * 16],
            in_=rts[:, MK:MK + 256].rearrange("p (t e) -> p t e", e=16),
            axis=mybir.AxisListType.X, op=ALU.max), [rtb], [rtb])
        if r_ < 3:
            vop(lambda v, r_=r_: v.tensor_tensor(
                out=rts[:, EQ:EQ + 256],
                in0=rts[:, MK:MK + 256].rearrange("p (t e) -> p t e", e=16),
                in1=rts[:, MX + r_ * 16:MX + (r_ + 1) * 16].broadcast_to(
                    (128, 16, 16)), op=ALU.is_equal), [rtb], [rtb])
            vop(lambda v: v.scalar_tensor_tensor(
                out=rts[:, MK:MK + 256], in0=rts[:, EQ:EQ + 256],
                scalar=-2e9, in1=rts[:, MK:MK + 256], op0=ALU.mult,
                op1=ALU.add), [rtb], [rtb])
    vop(lambda v: v.tensor_tensor(
        out=rts[:, EQ:EQ + 256],
        in0=comb[:].rearrange("p (t e) -> p t e", e=16),
        in1=rts[:, MX + 48:MX + 64].broadcast_to((128, 16, 16)), op=ALU.is_ge), [rtb, cmb], [rtb])
    vop(lambda v: v.tensor_tensor(out=rts[:, MK:MK + 256],
                                  in0=rts[:, EQ:EQ + 256], in1=sco[:],
                                  op=ALU.mult), [rtb, scob], [rtb])
    vop(lambda v: v.tensor_reduce(
        out=rts[:, THR:THR + 16],
        in_=rts[:, MK:MK + 256].rearrange("p (t e) -> p t e", e=16),
        axis=mybir.AxisListType.X, op=ALU.add), [rtb], [rtb])
    vop(lambda v: v.tensor_scalar(
        out=rts[:, THR + 16:THR + 32], in0=rts[:, THR:THR + 16],
        scalar1=1e-20, scalar2=None, op0=ALU.add), [rtb], [rtb])
    vop(lambda v: v.reciprocal(out=rts[:, THR + 32:THR + 48],
                               in_=rts[:, THR + 16:THR + 32]), [rtb], [rtb])
    vop(lambda v: v.tensor_tensor(
        out=comb[:], in0=rts[:, MK:MK + 256].rearrange("p (t e) -> p t e", e=16),
        in1=rts[:, THR + 32:THR + 48].broadcast_to((128, 16, 16)), op=ALU.mult), [rtb], [cmb])
    vop(lambda v: v.tensor_scalar(
        out=comb[:], in0=comb[:], scalar1=2.5, scalar2=None, op0=ALU.mult),
        [cmb], [cmb])

    yab = R.buf("y_acc")
    for e_ in range(NE):
        par = e_ % 2
        wgb, wub, wdb = (R.buf(("wg", par)), R.buf(("wu", par)),
                         R.buf(("wd", par)))
        p.add("gpsimd", lambda g, e_=e_, par=par: g.dma_start(
            out=wg_sb[par][:],
            in_=ap["wgt"][e_].rearrange("(kt p) i -> p kt i", p=128)),
            writes=[wgb], dma=True)
        p.add("gpsimd", lambda g, e_=e_, par=par: g.dma_start(
            out=wu_sb[par][:],
            in_=ap["wut"][e_].rearrange("(kt p) i -> p kt i", p=128)),
            writes=[wub], dma=True)
        p.add("gpsimd", lambda g, e_=e_, par=par: g.dma_start(
            out=wd_sb[par][:],
            in_=ap["wdt"][e_].rearrange("(it p) d -> p it d", p=128)),
            writes=[wdb], dma=True)
        for tc in range(4):
            hb = R.buf(("hT", tc % 2))
            for it in range(4):
                pgb, pub = Rp.buf(("pG", it % 2)), Rp.buf(("pU", it % 2))
                def mk_mm(w, pp, par=par, tc=tc, it=it):
                    def mm(t):
                        i = []
                        for kt in range(8):
                            i.append(t.matmul(
                                pp[it % 2][:],
                                w[par][:, kt * 512 + it * 128:
                                       kt * 512 + (it + 1) * 128],
                                x3T[:, kt * 2048 + tc * 512:
                                    kt * 2048 + (tc + 1) * 512],
                                start=(kt == 0), stop=(kt == 7),
                                skip_group_check=True))
                        return i
                    return mm
                p.add("tensor", mk_mm(wg_sb, pG), reads=[wgb, R.buf("x3T")],
                      writes=[pgb])
                p.add("tensor", mk_mm(wu_sb, pU), reads=[wub, R.buf("x3T")],
                      writes=[pub])
                slb = R.buf(("sil", it % 2))
                p.add("scalar", lambda s, it=it: s.activation(
                    out=sil_t[it % 2][:], in_=pG[it % 2][:], func=AF.Silu),
                    reads=[pgb], writes=[slb])
                p.add("vector", lambda v, it=it, tc=tc: v.tensor_tensor(
                    out=hT_sb[tc % 2][:, it * 512:(it + 1) * 512],
                    in0=sil_t[it % 2][:], in1=pU[it % 2][:], op=ALU.mult),
                    reads=[slb, pub], writes=[hb])
            for tw in range(4):
                tt = tc * 4 + tw
                for half in range(2):
                    pyb = Rp.buf(("pY", half))
                    def mmd(t, par=par, tc=tc, tw=tw, half=half):
                        i = []
                        for it in range(4):
                            i.append(t.matmul(
                                pY[half][:],
                                hT_sb[tc % 2][:, it * 512 + tw * 128:
                                              it * 512 + (tw + 1) * 128],
                                wd_sb[par][:, it * 1024 + half * 512:
                                           it * 1024 + (half + 1) * 512],
                                start=(it == 0), stop=(it == 3),
                                skip_group_check=True))
                        return i
                    p.add("tensor", mmd, reads=[hb, wdb], writes=[pyb])
                    if e_ == 0:
                        p.add("vector", lambda v, tt=tt, half=half, e_=e_:
                              v.tensor_scalar(
                                  out=y_acc[:, tt * 1024 + half * 512:
                                            tt * 1024 + (half + 1) * 512],
                                  in0=pY[half][:],
                                  scalar1=comb[:, tt * 16 + e_:tt * 16 + e_ + 1],
                                  scalar2=None, op0=ALU.mult),
                              reads=[pyb, cmb], writes=[yab])
                    else:
                        p.add("vector", lambda v, tt=tt, half=half, e_=e_:
                              v.scalar_tensor_tensor(
                                  out=y_acc[:, tt * 1024 + half * 512:
                                            tt * 1024 + (half + 1) * 512],
                                  in0=pY[half][:],
                                  scalar=comb[:, tt * 16 + e_:tt * 16 + e_ + 1],
                                  in1=y_acc[:, tt * 1024 + half * 512:
                                            tt * 1024 + (half + 1) * 512],
                                  op0=ALU.mult, op1=ALU.add),
                              reads=[pyb, cmb], writes=[yab])

    store_steps = []
    for tt in range(16):
        xrb = R.buf(("fx", tt % 2))
        p.add("sync", lambda e, tt=tt: e.dma_start(
            out=f32_t[tt % 2][:], in_=x3d[tt * 128:(tt + 1) * 128, :]),
            reads=[Rd.buf(tt)], writes=[xrb], dma=True)
        sb = xrb
        p.add("vector", lambda v, tt=tt: v.tensor_tensor(
            out=f32_t[tt % 2][:], in0=f32_t[tt % 2][:],
            in1=y_acc[:, tt * 1024:(tt + 1) * 1024], op=ALU.add),
            reads=[yab], writes=[sb], fence=True)
        outb = R.buf(("fo", tt % 2))
        _ln_emit(p, eps_sb[:], f32_t[tt % 2][:], sb,
                 f32_t[2 + tt % 2][:], outb,
                 f32_t[4][:], R.buf("sq"),
                 stat[:, (tt % 2) * 8:(tt % 2) * 8 + 8], R.buf(("st", tt % 2)),
                 extra_reads=[R.buf("eps")])
        stc = p.add("sync", lambda e, tt=tt: e.dma_start(
            out=x4_out[tt * 128:(tt + 1) * 128, :], in_=f32_t[2 + tt % 2][:]),
            reads=[outb], dma=True)
        store_steps.append(stc)
    p.add("gpsimd", lambda e: None, deps=store_steps)
    return ap, p, st


def _prep_l3(x2_full, y_full, inputs, b):
    return {
        "x2b": np.ascontiguousarray(x2_full[b]).astype(np.float32),
        "yb": np.ascontiguousarray(y_full[b]).astype(np.float32),
        "rwt": np.ascontiguousarray(
            np.asarray(inputs["L1_ffn_router_weight"]).T).astype(np.float32),
        "rbr": np.asarray(inputs["L1_ffn_router_bias"]).reshape(1, NE)
            .astype(np.float32),
        "wgt": np.ascontiguousarray(
            np.asarray(inputs["L1_ffn_w_gate"]).transpose(0, 2, 1))
            .astype(np.float32),
        "wut": np.ascontiguousarray(
            np.asarray(inputs["L1_ffn_w_up"]).transpose(0, 2, 1))
            .astype(np.float32),
        "wdt": np.ascontiguousarray(
            np.asarray(inputs["L1_ffn_w_down"]).transpose(0, 2, 1))
            .astype(np.float32),
        "identf": np.eye(128, dtype=np.float32),
        "onesf": np.ones((1, 128), np.float32),
        "epsc": np.full((128, 1), 1e-5, dtype=np.float32),
    }


def _run(build, in_maps):
    nc = bass.Bass("TRN2", target_bir_lowering=False)
    out = build(nc)
    p, st = out[-2], out[-1]
    with st:
        p.emit(nc, st)
    return run_bass_kernel_spmd(nc, in_maps, core_ids=list(range(NCORE)))


def kernel(**inputs):
    inputs = {k: np.asarray(v) for k, v in inputs.items()}

    res0 = _run(build_l0, [_prep_l0(inputs, b) for b in range(NCORE)])
    x2_full = np.stack([np.asarray(res0.results[b]["x2"])
                        for b in range(NCORE)])

    res2a = _run(build_l2a, [_prep_l2a(x2_full, inputs, k)
                             for k in range(NCORE)])
    Ball = np.concatenate([np.asarray(res2a.results[k]["Bout"])
                           for k in range(NCORE)])
    Mall = np.concatenate([np.asarray(res2a.results[k]["Mout"])
                           for k in range(NCORE)])

    res2b = _run(build_l2b, [_prep_l2b(x2_full, Ball, Mall, inputs, k)
                             for k in range(NCORE)])
    y_full = np.zeros((B, S, D), np.float32)
    for k in range(NCORE):
        yk = np.asarray(res2b.results[k]["yout"]).reshape(NCHUNK, CH, B, ESL)
        y_full[:, :, k * ESL:(k + 1) * ESL] = \
            yk.transpose(2, 0, 1, 3).reshape(B, S, ESL)

    res3 = _run(build_l3, [_prep_l3(x2_full, y_full, inputs, b)
                           for b in range(NCORE)])
    x4 = np.stack([np.asarray(res3.results[b]["x4"]) for b in range(NCORE)])
    return x4.astype(np.float32)
